# version 08
# baseline (speedup 1.0000x reference)
import contextlib
import numpy as np
import ml_dtypes

import concourse.bass as bass
import concourse.mybir as mybir
from concourse.bass_utils import run_bass_kernel_spmd

F32 = mybir.dt.float32
BF16 = mybir.dt.bfloat16
I32 = mybir.dt.int32
AF = mybir.ActivationFunctionType
OP = mybir.AluOpType
BF = ml_dtypes.bfloat16

HID = 4096
HT = 32
DIN = 8192
G = 8
CONV = 10240
PROJ = 18560
T = 1024
CS = 128
NCH = 8
EPS = 1e-5
NXT = 64
LN_INV = 1.0 / HID


def build(dbg=False, races=True):
    import os as _os0
    _GPV = bool(int(_os0.environ.get("K_GPV", "0")))
    nc = bass.Bass("TRN2", target_bir_lowering=False, detect_race_conditions=races)
    es = contextlib.ExitStack()

    def din(name, shape, dt=F32):
        return nc.dram_tensor(name, shape, dt, kind="ExternalInput").ap()

    xT = din("xT", [HID, T])
    wbT = din("wbT", [HID, HID])
    ipwT = din("ipwT", [HID, PROJ])
    opwT = din("opwT", [DIN, HID])
    aallT = din("aallT", [HID, 128])
    ballT = din("ballT", [128, HID])
    rwT = din("rwT", [HID, 16])
    rexp = din("rexp", [16, 128])
    bbcol = din("bbcol", [128, HT])
    prior_c = din("prior_c", [16, 1])
    rb_c = din("rb_c", [16, 1])
    iota16 = din("iota16", [1, 16])
    activ = din("activ", [1, 4], I32)
    identbf = din("identbf", [128, 128], BF16)
    identf = din("identf", [128, 128])
    onesbf = din("onesbf", [128, 1], BF16)
    onesrow = din("onesrow", [1, 128])
    ones16 = din("ones16", [16, 1])
    masknegf = din("masknegf", [128, 128])
    cwT = din("cwT", [128, 320])
    cbT = din("cbT", [128, 80])
    dtb_c = din("dtb_c", [128, 1])
    alog_c = din("alog_c", [128, 1])
    drep = din("drep", [128, NXT])
    nwT = din("nwT", [128, NXT])
    cmask = din("cmask", [128, 1])

    kindD = "ExternalOutput" if dbg else "Internal"
    x1sp = nc.dram_tensor("x1sp", [HT, 128, T], F32, kind=kindD).ap()
    x2sp = nc.dram_tensor("x2sp", [HT, 128, T], F32, kind=kindD).ap()
    gsp = nc.dram_tensor("gsp", [64, 128, T], BF16, kind=kindD).ap()
    xcsp = nc.dram_tensor("xcsp", [80, 128, T], BF16, kind=kindD).ap()
    xssp = nc.dram_tensor("xssp", [NCH, 128, DIN], BF16, kind=kindD).ap()
    plsp = nc.dram_tensor("plsp", [NCH, 128, DIN], BF16, kind=kindD).ap()
    psp = nc.dram_tensor("psp", [HT, 128, T], F32, kind="Internal").ap()
    ysp = nc.dram_tensor("ysp", [64, 128, T], BF16, kind=kindD).ap()
    ctxin = nc.dram_tensor("ctxin", [128, 240], BF16, kind="Internal")
    raw03_d = nc.dram_tensor("raw03_d", [128, 240], BF16, kind="Internal")
    bbh_d = nc.dram_tensor("bbh_d", [128, HT], F32, kind="Internal")
    ctxout = nc.dram_tensor("ctxout", [256, 240], BF16, kind="Internal")
    pstin = nc.dram_tensor("pstin", [128, DIN], F32, kind="Internal")
    pstout = nc.dram_tensor("pstout", [256, DIN], F32, kind="Internal")
    outT = nc.dram_tensor("outT", [HID, T], F32, kind="ExternalOutput").ap()

    xT3 = xT.rearrange("(a p) m -> a p m", p=128)
    out3 = outT.rearrange("(a p) m -> a p m", p=128)

    sb = {}
    _stat_off = [16512]
    _stat_pos = {}

    def stat(name, shape, dt):
        nbytes = int(np.prod(shape[1:])) * (2 if dt in (BF16,) else 4)
        off = (_stat_off[0] + 63) // 64 * 64
        _stat_off[0] = off + nbytes
        assert _stat_off[0] <= 65408, (name, _stat_off[0])
        _stat_pos[name] = off
        h = nc.alloc_sbuf_tensor_at(name, shape, dt, offset=off)
        sb[name] = h
        return h

    def at(name, shape, dt, off):
        h = nc.alloc_sbuf_tensor_at(name, shape, dt, offset=off)
        sb[name] = h
        return h

    st_identbf = stat("st_identbf", [128, 128], BF16)
    st_identf = stat("st_identf", [128, 128], F32)
    st_maskneg = stat("st_maskneg", [128, 128], F32)
    st_onesbf = stat("st_onesbf", [128, 1], BF16)
    st_onesrow = stat("st_onesrow", [1, 128], F32)
    st_ones16 = stat("st_ones16", [16, 1], F32)
    st_o16r = stat("st_o16r", [1, 16], F32)
    st_iota = stat("st_iota", [1, 16], F32)
    st_activf = stat("st_activf", [1, 4], F32)
    st_mrow = stat("st_mrow", [1, 16], F32)
    st_mcol = stat("st_mcol", [16, 1], F32)
    st_bias16 = stat("st_bias16", [16, 1], F32)
    st_scr16 = stat("st_scr16", [16, 1], F32)
    st_bbh = stat("st_bbh", [128, HT], F32)
    st_rexp = stat("st_rexp", [16, 128], F32)
    st_rw = stat("st_rw", [128, HT * 16], BF16)
    st_ball = stat("st_ball", [128, HID], BF16)
    st_cw = stat("st_cw", [128, 320], F32)
    st_cb = stat("st_cb", [128, 80], F32)
    st_dtb = stat("st_dtb", [128, 1], F32)
    st_alog = stat("st_alog", [128, 1], F32)
    st_acol = stat("st_acol", [128, 1], F32)
    st_drep = stat("st_drep", [128, NXT], F32)
    st_nw = stat("st_nw", [128, NXT], F32)
    st_cmask = stat("st_cmask", [128, 1], F32)
    st_dtv = stat("st_dtv", [128, T], F32)
    st_cum = stat("st_cum", [128, T], F32)
    st_cumT = stat("st_cumT", [128, T], F32)
    st_sdT = None
    st_dtok = None
    st_dtvT = None
    st_grow = stat("st_grow", [1, NCH * 128], F32)
    st_gpre = stat("st_gpre", [1, NCH * 128], F32)
    st_cend = stat("st_cend", [1, 256], F32)
    st_gnr = stat("st_gnr", [1, T], F32)
    st_presil = stat("st_presil", [128, 240], F32)
    st_ctxs = stat("st_ctxs", [128, 240], BF16)
    st_ctxr = stat("st_ctxr", [128, 240], BF16)
    st_patch = stat("st_patch", [128, 240], F32)
    st_patch2 = stat("st_patch2", [128, 240], F32)
    st_patchb = stat("st_patchb", [128, 240], BF16)
    st_raw03 = stat("st_raw03", [128, 240], BF16)
    st_konst = stat("st_konst", [128, 4], F32)

    A0, A1, AW = 65408, 130944, 196480

    xln = at("xln", [128, HT * 1024], BF16, A0)
    sca_xcw = [at(f"sca_xcw{i}", [128, 64 * 128], BF16, A0 + i * 16384) for i in range(2)]
    sca_xs = at("sca_xs", [128, DIN], BF16, A0 + 32768)
    sca_xs2 = at("sca_xs2", [128, DIN], BF16, A0 + 49152)
    sca_btok = at("sca_btok", [128, 8 * 128], BF16, AW)
    scb_gtw = at("scb_gtw", [128, DIN], BF16, A0)
    scb_xs = at("scb_xs", [128, DIN], BF16, A0 + 16384)
    scb_plrb = at("scb_plrb", [128, DIN], BF16, A0 + 32768)
    scb_ygt = at("scb_ygt", [128, DIN], BF16, A0 + 49152)

    a1_ball_src = at("a1_aall", [128, HT * 128], BF16, A1)
    st_zs = at("st_zs", [128, T], BF16, A1 + 8192)
    st_alpha = at("st_alpha", [16, T], F32, A1 + 10240)
    st_pmask = at("st_pmask", [16, T], F32, A1 + 14336)
    st_dtraw = at("st_dtraw", [128, T], F32, A1 + 18432)
    st_row = at("st_row", [1, T], F32, A1 + 22528)
    st_row2 = at("st_row2", [1, T], F32, A1 + 26624)
    st_rowt = at("st_rowt", [1, T], F32, A1 + 30720)
    stg_ev = [at(f"stg_ev{i}", [128, 512], F32, A1 + 34816 + i * 2048) for i in range(4)]
    xrow = [at(f"xrow{i}", [128, 1030], BF16, A1 + 43008 + i * 2112) for i in range(2)]
    diag4 = at("diag4", [128, 2 * 512], BF16, A1 + 47232)
    gstage = [at(f"gstage{i}", [128, 1024], BF16, A1 + 49280 + i * 2048) for i in range(2)]
    stg_bb = [at(f"stg_bb{i}", [128, 2], F32, A1 + 53376 + i * 64) for i in range(2)]
    stg_ball = [at(f"stg_ball{i}", [128, 256], BF16, A1 + 53504 + i * 512) for i in range(2)]
    stg_cw = [at(f"stg_cw{i}", [128, 8], F32, A1 + 54528 + i * 64) for i in range(2)]
    stg_cb = [at(f"stg_cb{i}", [128, 2], F32, A1 + 54656 + i * 64) for i in range(2)]
    stg_smb = at("stg_smb", [128, 12], BF16, A1 + 54784)
    stg_sm6 = at("stg_sm6", [128, 12], BF16, A1 + 54848)
    stg_x = [at(f"stg_x{i}", [128, 1024], BF16, A1 + 55296 + i * 2048) for i in range(4)]
    stg_sq = [at(f"stg_sq{i}", [128, 1024], BF16, A1 + 63488 + i * 2048) for i in range(1)]
    scb_xcw = at("scb_xcw", [128, 64 * 128], BF16, A1)
    scb_pini = at("scb_pini", [128, DIN], BF16, A1 + 16384)
    scb_tmp = at("scb_tmp", [128, DIN], BF16, A1 + 32768)
    scb_t8 = at("scb_t8", [128, 512], F32, AW + 8192)
    scb_gmt = [at(f"scb_gmt{i}", [128, 128], BF16, AW + 10240 + i * 256) for i in range(2)]
    scb_lpre = [at(f"scb_lpre{i}", [128, 512], F32, AW + 10752 + i * 2048) for i in range(2)]
    scb_lt = [at(f"scb_lt{i}", [128, 512], BF16, AW + 14848 + i * 1024) for i in range(2)]
    scb_gml = [at(f"scb_gml{i}", [128, 512], BF16, AW + 16896 + i * 1024) for i in range(2)]
    sca_p = [at(f"sca_p{i}", [128, DIN], F32, A1 + i * 32768) for i in range(2)]

    ws = [at(f"ws{i}", [128, 32 * 256], BF16, AW + i * 16384) for i in range(2)]
    st_sdT = at("st_sdT", [128, T], F32, AW + 19456)
    st_dtok = at("st_dtok", [128, T], F32, AW + 23552)
    st_dtvT = at("st_dtvT", [128, T], BF16, AW + 27648)
    st_dtv2T = at("st_dtv2T", [128, T], BF16, AW + 29696)
    st_gamT = at("st_gamT", [128, T], F32, _stat_pos["st_dtv"])
    scb_bcw = at("scb_bcw", [128, 16 * 128], BF16, AW)
    sca_bw = [at(f"sca_bw{i}", [128, 8 * 128], BF16, AW + 4096 + i * 2048) for i in range(2)]
    assert AW + 32768 <= 229376

    psL = es.enter_context(nc.psum_tensor("psL", [128, 2048], F32))
    psH = es.enter_context(nc.psum_tensor("psH", [128, 2048], F32))

    class _PB:
        __slots__ = ("h", "k")

        def __init__(self, h, k):
            self.h = h
            self.k = k

        def ap(self):
            return self.h.ap()[:, self.k * 512:(self.k + 1) * 512]

        def __getitem__(self, sl):
            return self.ap()[sl]

    pb = [_PB(psL, i) for i in range(4)] + [_PB(psH, i) for i in range(4)]

    sems = []

    def mksem(name):
        h = es.enter_context(nc.semaphore(name))
        sems.append(h)
        return h

    s_cst = [mksem(f"s_cst{i}") for i in range(4)]
    s_wl = [mksem(f"s_wl{i}") for i in range(2)]
    s_xl = [mksem(f"s_xl{i}") for i in range(2)]
    s_sp = [mksem(f"s_sp{i}") for i in range(2)]
    s_rb = [mksem(f"s_rb{i}") for i in range(4)]
    s_rc = [mksem(f"s_rc{i}") for i in range(2)]
    s_rg = mksem("s_rg")
    s_st = [mksem(f"s_st{i}") for i in range(2)]
    s_cc = mksem("s_cc")
    c_sd = mksem("c_sd")
    c_ev = mksem("c_ev")
    c_se = mksem("c_se")
    c_ve = mksem("c_ve")
    c_ln = mksem("c_ln")
    c_pe = mksem("c_pe")
    c_cv = mksem("c_cv")
    v_self = mksem("v_self")
    s_self = mksem("s_self")

    cnt = {id(h): 0 for h in sems}

    def inc(inst, s, k=1):
        inst.then_inc(s, k)
        cnt[id(s)] += k

    def bump(eng, s, k=1):
        eng.sem_inc(s, k)
        cnt[id(s)] += k

    def C(s):
        return cnt[id(s)]

    def ss(eng, i, sem):
        inc(i, sem, 1)
        eng.wait_ge(sem, C(sem))

    def aw(eng, r, mul_s2, addc, sem):
        assert addc >= 0, addc
        eng.reg_mov(r, mul_s2[0])
        eng.reg_mul(r, r, mul_s2[1])
        if addc:
            eng.reg_add(r, r, addc)
        eng.wait_ge(sem, r)

    with nc.Block("pk") as blk:
        @blk.vector
        def _(v):
            v.memset(st_konst.ap()[:, 0:1], 1e-12)
            v.memset(st_konst.ap()[:, 1:2], EPS)
            i = v.memset(st_konst.ap()[:, 2:3], 1.0)
            inc(i, c_ve)
    with nc.Block("p0") as blk:
        @blk.sync
        def _(sy):
            loads = [(st_identbf, identbf), (st_identf, identf), (st_maskneg, masknegf),
                     (st_onesbf, onesbf), (st_onesrow, onesrow), (st_ones16, ones16),
                     (st_iota, iota16), (st_rexp, rexp), (st_cw, cwT), (st_cb, cbT),
                     (st_dtb, dtb_c), (st_alog, alog_c), (st_drep, drep), (st_nw, nwT),
                     (st_cmask, cmask), (st_scr16, prior_c), (st_mcol, rb_c), (st_bbh, bbcol),
                     (st_o16r, iota16)]
            for d, s in loads:
                i = sy.dma_start(out=d[:], in_=s)
                inc(i, s_xl[0], 16)

        @blk.gpsimd
        def _(g):
            i = g.dma_start(out=st_activf[:], in_=activ)
            inc(i, s_cst[0], 16)
            i = g.dma_start(out=st_rw.ap().rearrange("p (a m) -> p a m", a=HT),
                            in_=rwT.rearrange("(a p) m -> p a m", p=128))
            inc(i, s_cst[0], 16)
            for h in range(2):
                i = g.dma_start(out=a1_ball_src.ap().rearrange("p (a m) -> p a m", a=HT)[:, 16 * h:16 * (h + 1), :],
                                in_=aallT.rearrange("(a p) m -> p a m", p=128)[:, 16 * h:16 * (h + 1), :])
                inc(i, s_cst[0], 16)
            i = g.dma_start(out=st_ball[:], in_=ballT)
            inc(i, s_cst[0], 16)

        @blk.scalar
        def _(s):
            s.wait_ge(s_xl[0], C(s_xl[0]))
            s.wait_ge(s_cst[0], C(s_cst[0]))
            s.activation(out=st_acol[:], in_=st_alog[:], func=AF.Exp)
            s.wait_ge(c_ve, 1)
            s.activation(out=st_bias16[:], in_=st_scr16[:], func=AF.Ln, bias=st_konst.ap()[0:16, 0:1])
            i = s.mul(st_bbh[:], st_bbh[:], 0.5)
            inc(i, c_se)

        @blk.vector
        def _(v):
            v.wait_ge(c_se, C(c_se))
            v.tensor_add(out=st_bias16[:], in0=st_bias16[:], in1=st_mcol[:])
            i = v.memset(st_mrow[:], 0.0)
            ss(v, i, v_self)
            for j in range(4):
                i = v.tensor_scalar(out=st_o16r[:], in0=st_iota[:], scalar1=st_activf[0:1, j:j + 1],
                                    scalar2=None, op0=OP.is_equal)
                ss(v, i, v_self)
                i = v.tensor_add(out=st_mrow[:], in0=st_mrow[:], in1=st_o16r[:])
                if j < 3:
                    ss(v, i, v_self)
            inc(i, c_ve)

        @blk.tensor
        def _(t):
            t.wait_ge(c_ve, C(c_ve))
            i = t.transpose(pb[0].ap()[0:16, 0:1], st_mrow[:], st_identf.ap()[0:1, 0:1])
            inc(i, c_pe)

    with nc.Block("p0b") as blk:
        @blk.vector
        def _(v):
            v.wait_ge(c_pe, C(c_pe))
            v.tensor_copy(out=st_mcol[:], in_=pb[0].ap()[0:16, 0:1])
            v.memset(st_o16r[:], 1.0)

        @blk.sync
        def _(sy):
            sy.wait_ge(c_se, C(c_se))
            i = sy.dma_start(out=bbh_d[:, :], in_=st_bbh[:])
            inc(i, s_rb[0], 16)
            sy.wait_ge(s_rb[0], C(s_rb[0]))
    nc.all_engine_barrier()

    def emit_ln(tag, src3, dstbuf):
        b_cst = [C(s) for s in s_cst]
        b_ln, b_se, b_pe, b_ve = C(c_ln), C(c_se), C(c_pe), C(c_ve)
        with nc.Block(f"ln_{tag}") as blk:
            @blk.gpsimd
            def _(g):
                for p in range(2):
                    for a in range(HT):
                        sl = a % 4
                        n = p * HT + a
                        if n >= 4:
                            g.wait_ge(c_ln, b_ln + n - 3)
                        i = g.dma_start(out=stg_x[sl][:], in_=src3[a])
                        inc(i, s_cst[sl], 16)

            @blk.scalar
            def _(s):
                for a in range(HT):
                    sl = a % 4
                    s.wait_ge(s_cst[sl], b_cst[sl] + 16 * (a // 4 + 1))
                    if a >= 1:
                        s.wait_ge(c_ln, b_ln + a)
                    i = s.activation(out=stg_sq[0][:], in_=stg_x[sl][:], func=AF.Square)
                    inc(i, c_se)
                s.wait_ge(c_ve, b_ve + 1)
                i = s.activation(out=st_row2[:], in_=st_row2[:], func=AF.Ln, bias=st_konst.ap()[0:1, 1:2])
                ss(s, i, s_self)
                i = s.activation(out=st_row2[:], in_=st_row2[:], func=AF.Exp, scale=-0.5)
                inc(i, c_se)

            @blk.tensor
            def _(t):
                for a in range(HT):
                    sl = a % 4
                    t.wait_ge(s_cst[sl], b_cst[sl] + 16 * (a // 4 + 1))
                    t.wait_ge(c_se, b_se + a + 1)
                    for ts in range(2):
                        t.matmul(pb[ts].ap()[0:1, :], st_onesbf[:], stg_x[sl][:, ts * 512:(ts + 1) * 512],
                                 start=(a == 0), stop=(a == HT - 1))
                    for ts in range(2):
                        i = t.matmul(pb[2 + ts].ap()[0:1, :], st_onesbf[:], stg_sq[0][:, ts * 512:(ts + 1) * 512],
                                     start=(a == 0), stop=(a == HT - 1))
                    inc(i, c_ln)
                t.wait_ge(c_se, b_se + HT + 1)
                for ts in range(2):
                    t.matmul(pb[4 + ts][:], st_onesrow[:], st_row.ap()[0:1, ts * 512:(ts + 1) * 512],
                             start=True, stop=True)
                for ts in range(2):
                    i = t.matmul(pb[6 + ts][:], st_onesrow[:], st_row2.ap()[0:1, ts * 512:(ts + 1) * 512],
                                 start=True, stop=True)
                inc(i, c_pe)

            @blk.vector
            def _(v):
                v.wait_ge(c_ln, b_ln + HT)
                v.tensor_scalar(out=st_row.ap()[0:1, 0:512], in0=pb[0].ap()[0:1, :], scalar1=LN_INV, scalar2=None, op0=OP.mult)
                v.tensor_scalar(out=st_row.ap()[0:1, 512:1024], in0=pb[1].ap()[0:1, :], scalar1=LN_INV, scalar2=None, op0=OP.mult)
                v.tensor_scalar(out=st_row2.ap()[0:1, 0:512], in0=pb[2].ap()[0:1, :], scalar1=LN_INV, scalar2=None, op0=OP.mult)
                i = v.tensor_scalar(out=st_row2.ap()[0:1, 512:1024], in0=pb[3].ap()[0:1, :], scalar1=LN_INV, scalar2=None, op0=OP.mult)
                ss(v, i, v_self)
                i = v.tensor_mul(out=st_rowt[:], in0=st_row[:], in1=st_row[:])
                ss(v, i, v_self)
                i = v.tensor_sub(out=st_row2[:], in0=st_row2[:], in1=st_rowt[:])
                inc(i, c_ve)
                v.wait_ge(c_pe, b_pe + 1)
                for a in range(HT):
                    sl = a % 4
                    n = HT + a
                    v.wait_ge(s_cst[sl], b_cst[sl] + 16 * (n // 4 + 1))
                    for ts in range(2):
                        i = v.tensor_sub(out=stg_ev[ts][:], in0=stg_x[sl][:, ts * 512:(ts + 1) * 512],
                                         in1=pb[4 + ts].ap()[:])
                    ss(v, i, v_self)
                    for ts in range(2):
                        i = v.tensor_mul(out=dstbuf.ap()[:, a * 1024 + ts * 512: a * 1024 + (ts + 1) * 512],
                                         in0=stg_ev[ts][:], in1=pb[6 + ts].ap()[:])
                    inc(i, c_ln)
        nc.all_engine_barrier()

    def emit_aux(tag, reload_a=False):
        b_pe, b_ve, b_se = C(c_pe), C(c_ve), C(c_se)
        with nc.Block(f"aux_{tag}") as blk:
            if reload_a:
                @blk.gpsimd
                def _(g):
                    for h in range(2):
                        i = g.dma_start(out=a1_ball_src.ap().rearrange("p (a m) -> p a m", a=HT)[:, 16 * h:16 * (h + 1), :],
                                        in_=aallT.rearrange("(a p) m -> p a m", p=128)[:, 16 * h:16 * (h + 1), :])
                        inc(i, s_cst[1], 16)

            @blk.tensor
            def _(t):
                if reload_a:
                    t.wait_ge(s_cst[1], C(s_cst[1]))
                for ts in range(2):
                    for k in range(HT):
                        t.matmul(pb[ts][:], a1_ball_src[:, k * 128:(k + 1) * 128],
                                 xln[:, k * 1024 + ts * 512: k * 1024 + ts * 512 + 512],
                                 start=(k == 0), stop=(k == HT - 1))
                for ts in range(2):
                    for k in range(HT):
                        i = t.matmul(pb[2 + ts].ap()[0:16, :], st_rw[:, k * 16:(k + 1) * 16],
                                     xln[:, k * 1024 + ts * 512: k * 1024 + ts * 512 + 512],
                                     start=(k == 0), stop=(k == HT - 1))
                inc(i, c_pe)
                t.wait_ge(c_ve, b_ve + 1)
                for ts in range(2):
                    i = t.matmul(pb[4 + ts].ap()[0:1, :], st_ones16[:],
                                 st_pmask[:, ts * 512:(ts + 1) * 512], start=True, stop=True)
                inc(i, c_pe)
                t.wait_ge(c_ve, b_ve + 2)
                for ts in range(2):
                    i = t.matmul(pb[6 + ts].ap()[0:16, :],
                                 st_onesrow.ap()[0:1, 0:16], st_rowt.ap()[0:1, ts * 512:(ts + 1) * 512],
                                 start=True, stop=True)
                inc(i, c_pe)
                t.wait_ge(c_ve, b_ve + 3)
                for ts in range(2):
                    i = t.matmul(pb[ts][:], st_rexp[:], st_alpha[:, ts * 512:(ts + 1) * 512],
                                 start=True, stop=True)
                inc(i, c_pe)

            @blk.scalar
            def _(s):
                s.wait_ge(c_pe, b_pe + 1)
                for ts in range(2):
                    i = s.activation(out=st_pmask[:, ts * 512:(ts + 1) * 512], in_=pb[2 + ts].ap()[0:16, :],
                                     func=AF.Exp, bias=st_bias16[:])
                inc(i, c_se)
                for ts in range(2):
                    i = s.copy(out=stg_ev[2 + ts][:], in_=pb[ts].ap()[:])
                inc(i, c_se)

            @blk.vector
            def _(v):
                v.wait_ge(c_se, b_se + 1)
                i = v.tensor_scalar(out=st_pmask[:], in0=st_pmask[:], scalar1=st_mcol[:], scalar2=None, op0=OP.mult)
                inc(i, c_ve)
                v.wait_ge(c_pe, b_pe + 2)
                for ts in range(2):
                    i = v.reciprocal(out=st_rowt.ap()[0:1, ts * 512:(ts + 1) * 512], in_=pb[4 + ts].ap()[0:1, :])
                inc(i, c_ve)
                v.wait_ge(c_pe, b_pe + 3)
                v.wait_ge(c_se, b_se + 2)
                for ts in range(2):
                    i = v.tensor_mul(out=st_alpha[:, ts * 512:(ts + 1) * 512],
                                     in0=st_pmask[:, ts * 512:(ts + 1) * 512], in1=pb[6 + ts].ap()[0:16, :])
                inc(i, c_ve)
                v.wait_ge(c_pe, b_pe + 4)
                for ts in range(2):
                    i = v.tensor_mul(out=st_zs[:, ts * 512:(ts + 1) * 512], in0=stg_ev[2 + ts][:],
                                     in1=pb[ts].ap()[:])
                inc(i, c_ve)
        nc.all_engine_barrier()

    def stg_r(q):
        if q < 4:
            return stg_x[q].ap().bitcast(F32)[:, 0:512]
        if q == 4:
            return stg_sq[0].ap().bitcast(F32)[:, 0:512]
        if q < 7:
            return gstage[q - 5].ap().bitcast(F32)[:, 0:512]
        return xrow[0].ap().bitcast(F32)[:, 0:512]

    op_ps = at("op_ps", [128, 8 * 512], F32, A1)
    op_x1 = at("op_x1", [128, 8 * 512], F32, A1 + 16384)

    def ops_ps(q):
        return op_ps.ap()[:, q * 512:(q + 1) * 512]

    def ops_x1(q):
        return op_x1.ap()[:, q * 512:(q + 1) * 512]

    def emit_ffn(tag, wdram, res3, dst3, pre=None):
        NS = 16
        wre = wdram.rearrange("(a p) m -> p a m", p=128)
        b_sd, b_se, b_ve = C(c_sd), C(c_se), C(c_ve)
        b_wl = pre if pre is not None else [C(s) for s in s_wl]
        b_st = [C(s) for s in s_st]
        b_xl = [C(s) for s in s_xl]
        b_sp = [C(s) for s in s_sp]

        def wload(g, u_expr, par):
            for p in range(4):
                g.dma_start(out=ws[par].ap().rearrange("p (a m) -> p a m", a=32)[:, 8 * p:8 * (p + 1), :],
                            in_=wre[:, 8 * p:8 * (p + 1), bass.ds(u_expr * 256, 256)]).then_inc(s_wl[par], 16)
            g.dma_start(out=stg_ball[par][:], in_=ballT[:, bass.ds(u_expr * 256, 256)]).then_inc(s_wl[par], 16)

        def staging(sy, u_expr, par):
            sy.dma_start(out=stg_bb[par][:], in_=bbh_d.ap()[:, bass.ds(u_expr * 2, 2)]).then_inc(s_xl[par], 16)
            for j in range(2):
                for ts in range(2):
                    sy.dma_start(out=stg_r(par * 4 + j * 2 + ts),
                                 in_=res3[bass.ds(u_expr * 2 + j, 1), :, ts * 512:(ts + 1) * 512].squeeze(0)).then_inc(s_xl[par], 16)

        def stores(sy, u_expr, par):
            for j in range(2):
                for ts in range(2):
                    sy.dma_start(out=dst3[bass.ds(u_expr * 2 + j, 1), :, ts * 512:(ts + 1) * 512].squeeze(0),
                                 in_=stg_r(par * 4 + j * 2 + ts)).then_inc(s_sp[par], 16)

        with nc.Block(f"ffn_{tag}") as blk:
            @blk.gpsimd
            def _(g):
                def wl_static(u, par):
                    for p in range(4):
                        g.dma_start(out=ws[par].ap().rearrange("p (a m) -> p a m", a=32)[:, 8 * p:8 * (p + 1), :],
                                    in_=wre[:, 8 * p:8 * (p + 1), u * 256:u * 256 + 256]).then_inc(s_wl[par], 16)
                    g.dma_start(out=stg_ball[par][:], in_=ballT[:, u * 256:u * 256 + 256]).then_inc(s_wl[par], 16)
                    cnt[id(s_wl[par])] += 80
                if pre is None:
                    for u in range(2):
                        wl_static(u, u)
                for u in range(2, NS):
                    par = u % 2
                    g.wait_ge(c_sd, b_sd + u - 1)
                    wl_static(u, par)

            @blk.sync
            def _(sy):
                def staging(u, par):
                    sy.dma_start(out=stg_bb[par][:], in_=bbh_d.ap()[:, u * 2:u * 2 + 2]).then_inc(s_xl[par], 16)
                    for j in range(2):
                        for ts in range(2):
                            sy.dma_start(out=stg_r(par * 4 + j * 2 + ts),
                                         in_=res3[u * 2 + j][:, ts * 512:(ts + 1) * 512]).then_inc(s_xl[par], 16)
                    cnt[id(s_xl[par])] += 80

                def stores(u, par):
                    for j in range(2):
                        for ts in range(2):
                            sy.dma_start(out=dst3[u * 2 + j][:, ts * 512:(ts + 1) * 512],
                                         in_=stg_r(par * 4 + j * 2 + ts)).then_inc(s_sp[par], 16)
                    cnt[id(s_sp[par])] += 64

                for u in range(2):
                    staging(u, u)
                for u in range(NS):
                    par = u % 2
                    sy.wait_ge(c_ve, b_ve + 4 * u + 4)
                    stores(u, par)
                    if u + 2 < NS:
                        staging(u + 2, par)
                sy.wait_ge(s_sp[0], b_sp[0] + 64 * (NS // 2))
                sy.wait_ge(s_sp[1], b_sp[1] + 64 * (NS // 2))

            @blk.tensor
            def _(t):
                bump(t, c_se, 4)
                b_se2 = b_se + 4
                with t.register(f"t{tag}") as r:
                    with t.Fori(0, NS // 2) as s2:
                        for par in range(2):
                            aw(t, r, (s2, 80), b_wl[par] + 80, s_wl[par])
                            aw(t, r, (s2, 8), b_se2 + 4 * par - 4, c_se)
                            for j in range(2):
                                for k in range(32):
                                    for ts in range(2):
                                        t.matmul(pb[par * 4 + j * 2 + ts][:],
                                                 ws[par][:, k * 256 + j * 128: k * 256 + j * 128 + 128],
                                                 xln[:, k * 1024 + ts * 512: k * 1024 + ts * 512 + 512],
                                                 start=(k == 0), stop=False)
                                for ts in range(2):
                                    mm = t.matmul(pb[par * 4 + j * 2 + ts][:],
                                                  stg_ball[par][:, j * 128:(j + 1) * 128],
                                                  st_zs[:, ts * 512:(ts + 1) * 512],
                                                  start=False, stop=True)
                            mm.then_inc(c_sd, 1)
                cnt[id(c_sd)] += NS

            @blk.scalar
            def _(s):
                with s.register(f"s{tag}") as r:
                    with s.Fori(0, NS // 2) as s2:
                        for par in range(2):
                            aw(s, r, (s2, 2), b_sd + par + 1, c_sd)
                            aw(s, r, (s2, 8), b_ve + 4 * par, c_ve)
                            aw(s, r, (s2, 80), b_xl[par] + 16, s_xl[par])
                            for j in range(2):
                                for ts in range(2):
                                    i = s.activation(out=stg_ev[j * 2 + ts][:], in_=pb[par * 4 + j * 2 + ts].ap()[:],
                                                     func=AF.Identity, bias=stg_bb[par][:, j:j + 1], scale=0.5)
                                    i.then_inc(c_se, 1)
                cnt[id(c_se)] += NS * 4

            @blk.vector
            def _(v):
                with v.register(f"v{tag}") as r:
                    with v.Fori(0, NS // 2) as s2:
                        for par in range(2):
                            for j in range(2):
                                for ts in range(2):
                                    q = j * 2 + ts
                                    aw(v, r, (s2, 8), b_se + 4 + 4 * par + q + 1, c_se)
                                    aw(v, r, (s2, 80), b_xl[par] + 16 + 16 * q + 16, s_xl[par])
                                    i = v.tensor_add(out=stg_r(par * 4 + q), in0=stg_ev[q][:], in1=stg_r(par * 4 + q))
                                    i.then_inc(c_ve, 1)
                cnt[id(c_ve)] += NS * 4
        nc.all_engine_barrier()

    def emit_ip_gate(pre=None):
        NS = 32
        b_sd, b_se = C(c_sd), C(c_se)
        b_wl = pre if pre is not None else [C(s) for s in s_wl]

        def wload(g, u_expr, par):
            for p in range(4):
                g.dma_start(out=ws[par].ap().rearrange("p (a m) -> p a m", a=32)[:, 8 * p:8 * (p + 1), :],
                            in_=ipw3[:, 8 * p:8 * (p + 1), bass.ds(u_expr * 256, 256)]).then_inc(s_wl[par], 16)

        with nc.Block("ipg") as blk:
            @blk.gpsimd
            def _(g):
                def wl_static(u, par):
                    for p in range(4):
                        g.dma_start(out=ws[par].ap().rearrange("p (a m) -> p a m", a=32)[:, 8 * p:8 * (p + 1), :],
                                    in_=ipw3[:, 8 * p:8 * (p + 1), u * 256:u * 256 + 256]).then_inc(s_wl[par], 16)
                    cnt[id(s_wl[par])] += 64
                if pre is None:
                    for u in range(2):
                        wl_static(u, u)
                for u in range(2, NS):
                    g.wait_ge(c_sd, b_sd + u - 1)
                    wl_static(u, u % 2)

            @blk.tensor
            def _(t):
                bump(t, c_se, 4)
                with t.register("tipg") as r:
                    with t.Fori(0, NS // 2) as s2:
                        for par in range(2):
                            aw(t, r, (s2, 64), b_wl[par] + 64, s_wl[par])
                            aw(t, r, (s2, 4), b_se + 2 + 2 * par, c_se)
                            for j in range(2):
                                for k in range(32):
                                    for ts in range(2):
                                        mm = t.matmul(pb[par * 4 + j * 2 + ts][:],
                                                      ws[par][:, k * 256 + j * 128: k * 256 + j * 128 + 128],
                                                      xln[:, k * 1024 + ts * 512: k * 1024 + ts * 512 + 512],
                                                      start=(k == 0), stop=(k == 31))
                            mm.then_inc(c_sd, 1)
                cnt[id(c_sd)] += NS

            @blk.scalar
            def _(s):
                b_sp2 = [C(x) for x in s_sp]
                with s.register("sipg") as r:
                    with s.Fori(0, NS // 2) as s2:
                        for par in range(2):
                            aw(s, r, (s2, 2), b_sd + par + 1, c_sd)
                            if True:
                                aw(s, r, (s2, 32), b_sp2[1 - par] + 32 * par, s_sp[1 - par])
                            for j in range(2):
                                for ts in range(2):
                                    i = s.activation(out=gstage[j][:, ts * 512:(ts + 1) * 512],
                                                     in_=pb[par * 4 + j * 2 + ts].ap()[:], func=AF.Silu)
                                    if ts == 1:
                                        i.then_inc(c_se, 1)
                cnt[id(c_se)] += NS * 2

            @blk.sync
            def _(sy):
                for u in range(NS):
                    par = u % 2
                    for j in range(2):
                        sy.wait_ge(c_se, b_se + 4 + 2 * u + j + 1)
                        sy.dma_start(out=gsp[u * 2 + j], in_=gstage[j][:]).then_inc(s_sp[par], 16)
                        cnt[id(s_sp[par])] += 16
                sy.wait_ge(s_sp[0], C(s_sp[0]))
                sy.wait_ge(s_sp[1], C(s_sp[1]))
        nc.all_engine_barrier()

    def emit_ip_xbc():
        NV = 40
        b_sd, b_se, b_pe, b_cv, b_ve = C(c_sd), C(c_se), C(c_pe), C(c_cv), C(c_ve)
        b_wl = [C(s) for s in s_wl]
        b_st = [C(s) for s in s_st]
        b_sp = [C(s) for s in s_sp]

        def wload(g, v_expr, par):
            for p in range(4):
                g.dma_start(out=ws[par].ap().rearrange("p (a m) -> p a m", a=32)[:, 8 * p:8 * (p + 1), :],
                            in_=ipw3[:, 8 * p:8 * (p + 1), bass.ds(v_expr * 256 + 8192, 256)]).then_inc(s_wl[par], 16)

        with nc.Block("ipx") as blk:
            @blk.gpsimd
            def _(g):
                def wl_static(v, par):
                    for p in range(4):
                        g.dma_start(out=ws[par].ap().rearrange("p (a m) -> p a m", a=32)[:, 8 * p:8 * (p + 1), :],
                                    in_=ipw3[:, 8 * p:8 * (p + 1), v * 256 + 8192:v * 256 + 8448]).then_inc(s_wl[par], 16)
                    cnt[id(s_wl[par])] += 64
                for v in range(2):
                    wl_static(v, v)
                for v in range(2, NV):
                    g.wait_ge(c_sd, b_sd + v - 1)
                    wl_static(v, v % 2)

            @blk.vector
            def _(v_):
                v_.memset(xrow[0].ap()[:, 0:3], 0.0)
                i = v_.memset(xrow[1].ap()[:, 0:3], 0.0)
                inc(i, c_ve)
                with v_.register("vipx") as r:
                    with v_.Fori(0, NV) as v1:
                        aw(v_, r, (v1, 16), b_st[0] + 16, s_st[0])
                        aw(v_, r, (v1, 1), b_pe, c_pe)
                        for j in range(2):
                            i = v_.tensor_tensor(
                                out=diag4.ap().rearrange("p (j k m) -> p j k m", j=2, k=4)[:, j],
                                in0=st_identbf.ap().unsqueeze(1).broadcast_to([128, 4, 128]),
                                in1=stg_cw[0][:, j * 4:(j + 1) * 4].unsqueeze(2).broadcast_to([128, 4, 128]),
                                op=OP.mult)
                        i.then_inc(c_ve, 1)
                cnt[id(c_ve)] += NV

            @blk.sync
            def _(sy):
                sy.dma_start(out=stg_cw[0][:], in_=cwT[:, 0:8]).then_inc(s_st[0], 16)
                sy.dma_start(out=stg_cb[0][:], in_=cbT[:, 0:2]).then_inc(s_st[1], 16)
                cnt[id(s_st[0])] += 16
                cnt[id(s_st[1])] += 16
                for v1 in range(NV):
                    if v1 + 1 < NV:
                        sy.wait_ge(c_ve, b_ve + 1 + v1 + 1)
                        sy.dma_start(out=stg_cw[0][:], in_=cwT[:, (v1 + 1) * 8:(v1 + 1) * 8 + 8]).then_inc(s_st[0], 16)
                        cnt[id(s_st[0])] += 16
                    sy.wait_ge(c_cv, b_cv + 2 + v1 + 1)
                    if v1 + 1 < NV:
                        sy.dma_start(out=stg_cb[0][:], in_=cbT[:, (v1 + 1) * 2:(v1 + 1) * 2 + 2]).then_inc(s_st[1], 16)
                        cnt[id(s_st[1])] += 16
                    for j in range(2):
                        sy.dma_start(out=xcsp[v1 * 2 + j], in_=gstage[j][:]).then_inc(s_sp[0], 16)
                    cnt[id(s_sp[0])] += 32
                    sy.wait_ge(c_se, b_se + v1 + 1)
                    sy.dma_start(out=ctxin[:, v1 * 6:v1 * 6 + 6], in_=stg_smb[:, (v1 % 2) * 6:(v1 % 2) * 6 + 6]).then_inc(s_st[1], 16)
                    sy.dma_start(out=raw03_d[:, v1 * 6:v1 * 6 + 6], in_=stg_sm6[:, (v1 % 2) * 6:(v1 % 2) * 6 + 6]).then_inc(s_st[1], 16)
                    cnt[id(s_st[1])] += 32
                sy.wait_ge(s_sp[0], C(s_sp[0]))

            @blk.tensor
            def _(t):
                bump(t, c_cv, 2)
                b_cv2 = b_cv + 2
                mmpar = [0]

                def main_mms():
                    mm = None
                    for j in range(2):
                        for k in range(32):
                            for ts in range(2):
                                mm = t.matmul(pb[j * 2 + ts][:],
                                              ws[mmpar[0]][:, k * 256 + j * 128: k * 256 + j * 128 + 128],
                                              xln[:, k * 1024 + ts * 512: k * 1024 + ts * 512 + 512],
                                              start=(k == 0), stop=(k == 31))
                    mm.then_inc(c_sd, 1)

                def conv_mms():
                    mm = None
                    for j in range(2):
                        for ts in range(2):
                            for k in range(4):
                                mm = t.matmul(pb[4 + j * 2 + ts][:],
                                              diag4[:, j * 512 + k * 128: j * 512 + (k + 1) * 128],
                                              xrow[j].ap()[:, ts * 512 + k: ts * 512 + k + 512],
                                              start=(k == 0), stop=(k == 3))
                    mm.then_inc(c_pe, 1)

                with t.register("tipx") as r:
                    t.wait_ge(s_wl[0], b_wl[0] + 64)
                    main_mms()
                    with t.Fori(0, NV // 2 - 1) as s2:
                        mmpar[0] = 1
                        aw(t, r, (s2, 64), b_wl[1] + 64, s_wl[1])
                        aw(t, r, (s2, 2), b_se + 1, c_se)
                        main_mms()
                        aw(t, r, (s2, 2), b_ve + 1 + 1, c_ve)
                        aw(t, r, (s2, 2), b_cv2, c_cv)
                        conv_mms()
                        mmpar[0] = 0
                        aw(t, r, (s2, 64), b_wl[0] + 128, s_wl[0])
                        aw(t, r, (s2, 2), b_se + 2, c_se)
                        main_mms()
                        aw(t, r, (s2, 2), b_ve + 2 + 1, c_ve)
                        aw(t, r, (s2, 2), b_cv2 + 1, c_cv)
                        conv_mms()
                    mmpar[0] = 1
                    t.wait_ge(s_wl[1], b_wl[1] + 64 * (NV // 2))
                    t.wait_ge(c_se, b_se + NV - 1)
                    main_mms()
                    t.wait_ge(c_ve, b_ve + (NV - 1) + 1)
                    t.wait_ge(c_cv, b_cv2 + NV - 2)
                    conv_mms()
                    t.wait_ge(c_se, b_se + NV)
                    t.wait_ge(c_ve, b_ve + NV + 1)
                    t.wait_ge(c_cv, b_cv2 + NV - 1)
                    conv_mms()
                cnt[id(c_sd)] += NV
                cnt[id(c_pe)] += NV

            @blk.scalar
            def _(s):
                for v1 in range(NV):
                    s.wait_ge(c_pe, b_pe + v1)
                    s.wait_ge(c_sd, b_sd + v1 + 1)
                    if v1 >= 2:
                        s.wait_ge(s_st[1], b_st[1] + 16 + 48 * (v1 - 1))
                    for j in range(2):
                        for ts in range(2):
                            i = s.activation(out=xrow[j].ap()[:, 3 + ts * 512: 3 + (ts + 1) * 512],
                                             in_=pb[j * 2 + ts].ap()[:], func=AF.Copy)
                    ss(s, i, s_self)
                    sm_off = (v1 % 2) * 6
                    for j in range(2):
                        s.copy(out=stg_smb[:, sm_off + j * 3:sm_off + (j + 1) * 3], in_=xrow[j].ap()[:, 3 + 1021: 3 + 1024])
                        i = s.copy(out=stg_sm6[:, sm_off + j * 3:sm_off + (j + 1) * 3], in_=xrow[j].ap()[:, 3: 6])
                    i.then_inc(c_se, 1)
                    cnt[id(c_se)] += 1
                    s.wait_ge(c_pe, b_pe + v1 + 1)
                    if v1 >= 1:
                        s.wait_ge(s_sp[0], b_sp[0] + 32 * v1)
                    for j in range(2):
                        for ts in range(2):
                            i = s.activation(out=gstage[j][:, ts * 512:(ts + 1) * 512],
                                             in_=pb[4 + j * 2 + ts].ap()[:], func=AF.Silu,
                                             bias=stg_cb[0][:, j:j + 1])
                    i.then_inc(c_cv, 1)
                    cnt[id(c_cv)] += 1
        nc.all_engine_barrier()

    def emit_dt():
        with nc.Block("dt") as blk:
            @blk.gpsimd
            def _(g):
                for p in range(4):
                    i = g.dma_start(out=ws[0].ap().rearrange("p (a m) -> p a m", a=32)[:, 8 * p:8 * (p + 1), 0:128],
                                    in_=ipw3[:, 8 * p:8 * (p + 1), 18432:18560])
                    inc(i, s_wl[0], 16)

            @blk.tensor
            def _(t):
                t.wait_ge(s_wl[0], C(s_wl[0]))
                for ts in range(2):
                    for k in range(32):
                        i = t.matmul(pb[ts].ap()[:], ws[0][:, k * 256:k * 256 + 128],
                                     xln[:, k * 1024 + ts * 512: k * 1024 + ts * 512 + 512],
                                     start=(k == 0), stop=(k == 31))
                inc(i, c_pe)

            @blk.vector
            def _(v):
                v.wait_ge(c_pe, C(c_pe))
                v.tensor_copy(out=st_dtraw.ap()[:, 0:512], in_=pb[0].ap()[:])
                i = v.tensor_copy(out=st_dtraw.ap()[:, 512:1024], in_=pb[1].ap()[:])
                inc(i, c_ve)
        nc.all_engine_barrier()

        tmp = [sca_p[0].ap()[:, i * 1024:(i + 1) * 1024] for i in range(6)]
        tz = sca_p[1].ap()[:, 0:128]
        with nc.Block("dtp") as blk:
            @blk.scalar
            def _(s):
                i = s.activation(out=tmp[0], in_=st_dtraw[:], func=AF.Identity, bias=st_dtb[:])
                ss(s, i, s_self)
                i = s.activation(out=tmp[1], in_=tmp[0], func=AF.Abs)
                ss(s, i, s_self)
                i = s.activation(out=tmp[2], in_=tmp[1], func=AF.Exp, scale=-1.0)
                ss(s, i, s_self)
                i = s.activation(out=tmp[3], in_=tmp[2], func=AF.Ln, bias=st_konst.ap()[:, 2:3])
                inc(i, c_se)

            @blk.vector
            def _(v):
                v.wait_ge(c_se, C(c_se))
                i = v.tensor_scalar_max(out=tmp[4], in0=tmp[0], scalar1=0.0)
                ss(v, i, v_self)
                i = v.tensor_add(out=st_dtv[:], in0=tmp[4], in1=tmp[3])
                ss(v, i, v_self)
                i = v.tensor_scalar(out=tmp[5], in0=st_dtv[:], scalar1=st_acol[:], scalar2=-1.0,
                                    op0=OP.mult, op1=OP.mult)
                ss(v, i, v_self)
                i = v.memset(tz, 0.0)
                ss(v, i, v_self)
                for c in range(NCH):
                    i = v.tensor_tensor_scan(out=st_cum.ap()[:, c * 128:(c + 1) * 128],
                                             data0=sca_p[0].ap()[:, 5 * 1024 + c * 128: 5 * 1024 + (c + 1) * 128],
                                             data1=tz, initial=0.0, op0=OP.add, op1=OP.add)
                inc(i, c_ve)

            @blk.tensor
            def _(t):
                t.wait_ge(c_ve, C(c_ve))
                b_ve_t = C(c_ve)
                for c in range(NCH):
                    if c >= 2:
                        t.wait_ge(c_ve, b_ve_t + c - 1)
                    t.transpose(pb[c % 2].ap()[:, 0:128], st_cum.ap()[:, c * 128:(c + 1) * 128], st_identf[:])
                    i = t.transpose(pb[2 + c % 2].ap()[:, 0:128], st_dtv.ap()[:, c * 128:(c + 1) * 128], st_identf[:])
                    inc(i, c_pe)

            @blk.vector
            def _(v2):
                b_pe = C(c_pe) - NCH
                for c in range(NCH):
                    v2.wait_ge(c_pe, b_pe + c + 1)
                    v2.tensor_copy(out=st_cumT.ap()[:, c * 128:(c + 1) * 128], in_=pb[c % 2].ap()[:, 0:128])
                    i = v2.tensor_copy(out=st_dtvT.ap()[:, c * 128:(c + 1) * 128], in_=pb[2 + c % 2].ap()[:, 0:128])
                    inc(i, c_ve)
        nc.all_engine_barrier()

        with nc.Block("dtp2a") as blk:
            @blk.scalar
            def _(s):
                i = s.activation(out=st_sdT[:], in_=st_cumT[:], func=AF.Exp)
                inc(i, c_se)
        b2 = {"pe": C(c_pe), "se": C(c_se), "ve": C(c_ve)}

        def s_copy_pos(c):
            return 1 if c == 0 else (3 if c == 1 else 3 * c - 1)

        def s_exp_pos(c):
            return s_copy_pos(c) + 1

        def s_dtok_pos(j):
            return 3 * j + 7 if j <= 5 else 23 + (j - 6)

        with nc.Block("dtp2b") as blk:
            @blk.tensor
            def _(t):
                t.wait_ge(c_se, b2["se"])
                for c in range(NCH):
                    if c >= 2:
                        t.wait_ge(c_se, b2["se"] + s_exp_pos(c - 2))
                    i = t.transpose(pb[4 + c % 2].ap()[0:1, 0:128],
                                    st_cum.ap()[:, c * 128 + 127: c * 128 + 128], st_identf[:])
                    inc(i, c_pe)
                    if c >= 1:
                        t.wait_ge(c_se, b2["se"] + s_copy_pos(c - 1))
                        if c >= 3:
                            t.wait_ge(c_ve, b2["ve"] + c - 2)
                        i = t.matmul(pb[6 + (c - 1) % 2][:, 0:128], st_onesrow[:],
                                     st_cend.ap()[0:1, ((c - 1) % 2) * 128: ((c - 1) % 2) * 128 + 128],
                                     start=True, stop=True)
                        inc(i, c_pe)
                t.wait_ge(c_se, b2["se"] + s_copy_pos(NCH - 1))
                t.wait_ge(c_ve, b2["ve"] + NCH - 2)
                i = t.matmul(pb[6 + (NCH - 1) % 2][:, 0:128], st_onesrow[:],
                             st_cend.ap()[0:1, ((NCH - 1) % 2) * 128: ((NCH - 1) % 2) * 128 + 128],
                             start=True, stop=True)
                inc(i, c_pe)

            @blk.scalar
            def _(s2):
                def dtok(j):
                    s2.wait_ge(c_ve, b2["ve"] + j + 1)
                    i = s2.activation(out=st_dtok.ap()[:, j * 128:(j + 1) * 128],
                                      in_=sca_p[1].ap()[:, 1024 + (j % 2) * 128: 1024 + (j % 2) * 128 + 128],
                                      func=AF.Exp)
                    inc(i, c_se)
                for c in range(NCH):
                    s2.wait_ge(c_pe, b2["pe"] + 2 * c + (1 if c == 0 else 0))
                    i = s2.copy(out=st_cend.ap()[0:1, (c % 2) * 128:(c % 2) * 128 + 128],
                                in_=pb[4 + c % 2].ap()[0:1, 0:128])
                    inc(i, c_se)
                    i = s2.activation(out=st_grow.ap()[0:1, c * 128:(c + 1) * 128],
                                      in_=pb[4 + c % 2].ap()[0:1, 0:128], func=AF.Exp)
                    inc(i, c_se)
                    if c >= 2:
                        dtok(c - 2)
                dtok(NCH - 2)
                dtok(NCH - 1)
                assert C(c_se) == b2["se"] + 24

            @blk.vector
            def _(v):
                for c in range(NCH):
                    v.wait_ge(c_pe, b2["pe"] + (2 * c + 3 if c < NCH - 1 else 2 * NCH))
                    if c >= 2:
                        v.wait_ge(c_se, b2["se"] + s_dtok_pos(c - 2))
                    i = v.tensor_sub(out=sca_p[1].ap()[:, 1024 + (c % 2) * 128: 1024 + (c % 2) * 128 + 128],
                                     in0=pb[6 + c % 2].ap()[:, 0:128], in1=st_cumT.ap()[:, c * 128:(c + 1) * 128])
                    inc(i, c_ve)
        nc.all_engine_barrier()

        b3 = {"pe": C(c_pe)}
        with nc.Block("dtp3") as blk:
            @blk.vector
            def _(v):
                i = v.memset(st_gpre.ap()[0:1, 0:128], 1.0)
                ss(v, i, v_self)
                for c in range(1, NCH):
                    i = v.tensor_mul(out=st_gpre.ap()[0:1, c * 128:(c + 1) * 128],
                                     in0=st_gpre.ap()[0:1, (c - 1) * 128:c * 128],
                                     in1=st_grow.ap()[0:1, (c - 1) * 128:c * 128])
                    ss(v, i, v_self)
                i = v.tensor_tensor(out=st_dtv2T[:], in0=st_dtvT[:], in1=st_dtok[:], op=OP.mult)
                inc(i, c_ve)

            @blk.tensor
            def _(t):
                for half in range(2):
                    i = t.matmul(pb[2 + half].ap()[:, 0:512], st_onesrow[:],
                                 st_grow.ap()[0:1, half * 512:(half + 1) * 512],
                                 start=True, stop=True)
                    inc(i, c_pe)

            @blk.scalar
            def _(s):
                for half in range(2):
                    s.wait_ge(c_pe, b3["pe"] + half + 1)
                    i = s.copy(out=st_gamT.ap()[:, half * 512:(half + 1) * 512],
                               in_=pb[2 + half].ap()[:, 0:512])
                    inc(i, c_se)
        nc.all_engine_barrier()

    def emit_ctx():
        with nc.Block("ctx") as blk:
            @blk.sync
            def _(sy):
                i = sy.dma_start(out=st_raw03[:], in_=raw03_d[:, :])
                inc(i, s_rb[0], 16)
                sy.wait_ge(s_rb[0], C(s_rb[0]))

            @blk.gpsimd
            def _(g):
                g.wait_ge(s_rb[0], C(s_rb[0]))
                i = g.collective_compute("AllGather", OP.bypass, replica_groups=[[0, 1], [2, 3], [4, 5], [6, 7]],
                                         ins=[ctxin.ap().opt()], outs=[ctxout.ap().opt()])
                inc(i, s_cc, 1)

            @blk.sync
            def _(sy2):
                sy2.wait_ge(s_cc, C(s_cc))
                i = sy2.dma_start(out=st_ctxr[:], in_=ctxout[0:128, :])
                inc(i, s_rb[0], 16)

            @blk.vector
            def _(v):
                v.wait_ge(s_rb[0], C(s_rb[0]))
                i = v.tensor_scalar(out=st_ctxr[:], in0=st_ctxr[:], scalar1=st_cmask[:], scalar2=None, op0=OP.mult)
                inc(i, c_ve)
        nc.all_engine_barrier()

        def vw(tnsr, off, n):
            return tnsr.ap().rearrange("p (a q) -> p a q", a=80)[:, :, off:off + n]

        with nc.Block("patch") as blk:
            @blk.vector
            def _(v):
                cw4 = st_cw.ap().rearrange("p (a k) -> p a k", a=80)
                pview = st_patch.ap().rearrange("p (a q) -> p a q", a=80)
                p2view = st_patch2.ap().rearrange("p (a q) -> p a q", a=80)
                first = True
                for k in range(3):
                    n = 3 - k
                    wv = cw4[:, :, k:k + 1].broadcast_to([128, 80, n])
                    if first:
                        i = v.tensor_tensor(out=pview[:, :, 0:n], in0=vw(st_ctxr, k, n), in1=wv, op=OP.mult)
                        ss(v, i, v_self)
                        first = False
                    else:
                        i = v.tensor_tensor(out=p2view[:, :, 0:n], in0=vw(st_ctxr, k, n), in1=wv, op=OP.mult)
                        ss(v, i, v_self)
                        i = v.tensor_add(out=pview[:, :, 0:n], in0=pview[:, :, 0:n], in1=p2view[:, :, 0:n])
                        ss(v, i, v_self)
                for k in range(1, 4):
                    lo = max(0, 3 - k)
                    n = 3 - lo
                    if n <= 0:
                        continue
                    r0 = lo + k - 3
                    wv = cw4[:, :, k:k + 1].broadcast_to([128, 80, n])
                    i = v.tensor_tensor(out=p2view[:, :, lo:lo + n], in0=vw(st_raw03, r0, n), in1=wv, op=OP.mult)
                    ss(v, i, v_self)
                    i = v.tensor_add(out=pview[:, :, lo:lo + n], in0=pview[:, :, lo:lo + n], in1=p2view[:, :, lo:lo + n])
                    ss(v, i, v_self)
                cb3 = st_cb.ap().unsqueeze(2).broadcast_to([128, 80, 3])
                i = v.tensor_add(out=pview, in0=pview, in1=cb3)
                inc(i, c_ve)

            @blk.scalar
            def _(s):
                s.wait_ge(c_ve, C(c_ve))
                i = s.activation(out=st_patchb[:], in_=st_patch[:], func=AF.Silu)
                inc(i, c_se)
        nc.all_engine_barrier()

    def emit_sca():
        with nc.Block("scaz") as blk:
            @blk.vector
            def _(v):
                i = v.memset(sca_p[0][:], 0.0)
                inc(i, c_ve)
        nc.all_engine_barrier()

        B_pe = C(c_pe); B_ve = C(c_ve); B_se = C(c_se); B_cv = C(c_cv)
        B_rb = [C(s_rb[0]), C(s_rb[1])]; B_rc = [C(s_rc[0]), C(s_rc[1])]
        B_rb2 = C(s_rb[2]); B_cst = C(s_cst[0])

        def pe0(c):
            return B_pe + 88 * c

        def ev0(c):
            return B_ve + 48 * c + 1

        psH_g = psH.ap().bitcast(BF16).rearrange("p (k q m) -> p k q m", k=4, q=16, m=64)[:, :, 0:2, :]
        psL_bt = psL.ap().bitcast(BF16).rearrange("p (k q m) -> p k q m", k=4, q=8, m=128)[:, 2:4, 0, :]

        with nc.Block("sca") as blk:
            @blk.sync
            def _(sy):
                def rb(c):
                    w = c % 2
                    if c >= 2:
                        sy.wait_ge(c_pe, pe0(c - 2) + 64)
                    for p in range(8):
                        i = sy.dma_start(out=sca_xcw[w].ap().rearrange("p (a m) -> p a m", a=64)[:, 8 * p:8 * (p + 1), :],
                                         in_=xcsp.rearrange("a p m -> p a m")[:, 8 * p:8 * (p + 1), c * 128:(c + 1) * 128])
                        inc(i, s_rb[w], 16)
                    if c >= 2:
                        sy.wait_ge(c_pe, pe0(c - 2) + 72)
                    i = sy.dma_start(out=sca_bw[w].ap().rearrange("p (a m) -> p a m", a=8),
                                     in_=xcsp.rearrange("a p m -> p a m")[:, 64:72, c * 128:(c + 1) * 128])
                    inc(i, s_rc[w], 16)
                rb(0)
                rb(1)
                for c in range(NCH):
                    if c + 2 < NCH:
                        rb(c + 2)
                    sy.wait_ge(c_ve, ev0(c) + 32)
                    i = sy.dma_start(out=xssp[c], in_=sca_xs[:])
                    inc(i, s_rb[2], 16)
                sy.wait_ge(s_rb[2], B_rb2 + 16 * NCH)

            @blk.gpsimd
            def _(g):
                for c in range(NCH):
                    w = c % 2
                    if c >= 1:
                        g.wait_ge(c_ve, ev0(c - 1) + 48)
                        g.wait_ge(s_cst[0], B_cst + 16 * c)
                    i = g.dma_start(out=plsp[c], in_=sca_p[w][:])
                    inc(i, s_cst[0], 16)
                    if not _GPV:
                        i = g.tensor_tensor(
                            out=sca_p[1 - w].ap().rearrange("p (h m) -> p h m", h=128),
                            in0=sca_p[w].ap().rearrange("p (h m) -> p h m", h=128),
                            in1=st_gamT.ap()[:, c * 128:(c + 1) * 128].unsqueeze(2).broadcast_to([128, 128, 64]),
                            op=OP.mult)
                        inc(i, c_cv)
                g.wait_ge(s_cst[0], B_cst + 16 * NCH)

            @blk.tensor
            def _(t):
                for c in range(NCH):
                    w = c % 2
                    t.wait_ge(s_rb[w], B_rb[w] + 128 * (c // 2 + 1))
                    if c == 0:
                        t.wait_ge(c_ve, B_ve + 1)
                    else:
                        t.wait_ge(c_ve, ev0(c - 1) + 32)
                    for a in range(NXT):
                        if a >= 4 and a % 4 == 0:
                            t.wait_ge(c_ve, ev0(c) + 2 * ((a - 4) // 4 + 1))
                        i = t.transpose(pb[4 + a % 4].ap().bitcast(BF16)[:, 0:128],
                                        sca_xcw[w].ap()[:, a * 128:(a + 1) * 128], st_identbf[:])
                        inc(i, c_pe)
                    assert C(c_pe) == pe0(c) + 64
                    t.wait_ge(s_rc[w], B_rc[w] + 16 * (c // 2 + 1))
                    for gi in range(8):
                        if gi >= 2:
                            t.wait_ge(c_se, B_se + 4 * c + (gi - 2) // 2 + 1)
                        elif c >= 1:
                            t.wait_ge(c_se, B_se + 4 * c)
                        i = t.transpose(pb[2 + gi % 2].ap().bitcast(BF16)[:, 0:128],
                                        sca_bw[w].ap()[:, gi * 128:(gi + 1) * 128], st_identbf[:])
                        inc(i, c_pe)
                    assert C(c_pe) == pe0(c) + 72
                    for hb in range(16):
                        gi = hb // 2
                        t.wait_ge(c_se, B_se + 4 * c + gi // 2 + 1)
                        t.wait_ge(c_ve, ev0(c) + 2 * hb + 2)
                        if hb >= 2:
                            t.wait_ge(c_ve, ev0(c) + 32 + hb - 1)
                        for hq in range(8):
                            h = hb * 8 + hq
                            i = t.matmul(pb[hb % 2].ap()[:, hq * 64:(hq + 1) * 64],
                                         sca_btok.ap()[:, gi * 128:(gi + 1) * 128],
                                         sca_xs2.ap()[:, h * 64:(h + 1) * 64],
                                         start=True, stop=True)
                        inc(i, c_pe)
                    assert C(c_pe) == pe0(c) + 88

            @blk.vector
            def _(v):
                for c in range(NCH):
                    w = c % 2
                    if c == 0:
                        v.wait_ge(s_rb[0], B_rb[0] + 128)
                        v.wait_ge(s_rc[0], B_rc[0] + 16)
                        v.tensor_copy(out=sca_xcw[0].ap().rearrange("p (a m) -> p a m", a=64)[:, :, 0:3],
                                      in_=st_patchb.ap().rearrange("p (a q) -> p a q", a=80)[:, 0:64, :])
                        i = v.tensor_copy(out=sca_bw[0].ap().rearrange("p (a m) -> p a m", a=8)[:, :, 0:3],
                                          in_=st_patchb.ap().rearrange("p (a q) -> p a q", a=80)[:, 64:72, :])
                        inc(i, c_ve)
                    assert C(c_ve) == ev0(c)
                    if _GPV:
                        if c >= 1:
                            v.wait_ge(s_cst[0], B_cst + 16 * c)
                        i = v.tensor_tensor(
                            out=sca_p[1 - w].ap().rearrange("p (h m) -> p h m", h=128),
                            in0=sca_p[w].ap().rearrange("p (h m) -> p h m", h=128),
                            in1=st_gamT.ap()[:, c * 128:(c + 1) * 128].unsqueeze(2).broadcast_to([128, 128, 64]),
                            op=OP.mult)
                        i.then_inc(c_cv, 1)
                        cnt[id(c_cv)] += 1
                    for g in range(16):
                        if g == 0:
                            v.wait_ge(s_rb[2], B_rb2 + 16 * c)
                            if c >= 1:
                                v.wait_ge(c_pe, pe0(c - 1) + 88)
                        v.wait_ge(c_pe, pe0(c) + 4 * (g + 1))
                        i = v.tensor_tensor(
                            out=sca_xs.ap().rearrange("p (g k h m) -> p g k h m", g=16, k=4, h=2)[:, g],
                            in0=psH_g,
                            in1=st_dtvT.ap()[:, c * 128 + 8 * g: c * 128 + 8 * g + 8]
                                .rearrange("p (k h) -> p k h", k=4).unsqueeze(3).broadcast_to([128, 4, 2, 64]),
                            op=OP.mult)
                        inc(i, c_ve)
                        i = v.tensor_tensor(
                            out=sca_xs2.ap().rearrange("p (g k h m) -> p g k h m", g=16, k=4, h=2)[:, g],
                            in0=psH_g,
                            in1=st_dtv2T.ap()[:, c * 128 + 8 * g: c * 128 + 8 * g + 8]
                                .rearrange("p (k h) -> p k h", k=4).unsqueeze(3).broadcast_to([128, 4, 2, 64]),
                            op=OP.mult)
                        inc(i, c_ve)
                    assert C(c_ve) == ev0(c) + 32
                    for hb in range(16):
                        v.wait_ge(c_pe, pe0(c) + 72 + hb + 1)
                        if hb == 0:
                            v.wait_ge(c_cv, B_cv + c + 1)
                        i = v.tensor_add(out=sca_p[1 - w].ap()[:, hb * 512:(hb + 1) * 512],
                                         in0=sca_p[1 - w].ap()[:, hb * 512:(hb + 1) * 512],
                                         in1=pb[hb % 2].ap()[:])
                        inc(i, c_ve)
                    assert C(c_ve) == ev0(c) + 48

            @blk.scalar
            def _(s):
                for c in range(NCH):
                    for g in range(4):
                        s.wait_ge(c_pe, pe0(c) + 64 + 2 * (g + 1))
                        if c >= 1:
                            s.wait_ge(c_pe, pe0(c - 1) + 72 + 4 * g + 4)
                        i = s.copy(out=sca_btok.ap().rearrange("p (g z m) -> p g z m", g=4, z=2)[:, g],
                                   in_=psL_bt)
                        inc(i, c_se)
                    assert C(c_se) == B_se + 4 * (c + 1)
        nc.all_engine_barrier()

    def emit_pex():
        pfin = sca_p[0]
        with nc.Block("pex") as blk:
            @blk.sync
            def _(sy):
                i = sy.dma_start(out=pstin[:, :], in_=pfin[:])
                inc(i, s_rb[0], 16)
                sy.wait_ge(s_rb[0], C(s_rb[0]))
                scb_rb_xcw(sy, 0)
                scb_rb_gtw(sy, 0)

            @blk.gpsimd
            def _(g):
                scb_rb_small(g, 0)
                g.wait_ge(s_rb[0], C(s_rb[0]))
                i = g.collective_compute("AllGather", OP.bypass, replica_groups=[[0, 1], [2, 3], [4, 5], [6, 7]],
                                         ins=[pstin.ap().opt()], outs=[pstout.ap().opt()])
                inc(i, s_cc, 1)

            @blk.sync
            def _(sy2):
                sy2.wait_ge(s_cc, C(s_cc))
                i = sy2.dma_start(out=sca_p[1][:], in_=pstout[0:128, :])
                inc(i, s_rb[0], 16)

            @blk.vector
            def _(v):
                v.wait_ge(s_rb[0], C(s_rb[0]))
                i = v.tensor_scalar(out=sca_p[1][:], in0=sca_p[1][:], scalar1=st_cmask[:], scalar2=None, op0=OP.mult)
                inc(i, c_ve)

            @blk.scalar
            def _(s):
                s.wait_ge(c_ve, C(c_ve))
                i = s.copy(out=scb_pini[:], in_=sca_p[1][:])
                inc(i, c_se)
        nc.all_engine_barrier()

    def scb_marks():
        pe = 0
        gmt_mark = {}
        cumbc_mark = {}
        ydo_mark = {}
        for g in range(8):
            pe += 1; gmt_mark[g] = pe
            for qq in range(4):
                q = g * 4 + qq
                pe += 1; cumbc_mark[q] = pe
                if q >= 1:
                    pe += 1; ydo_mark[q - 1] = pe
        pe += 1; ydo_mark[31] = pe
        npe_head = pe
        ve = 0
        lpre_mark = {}
        y8_mark = {}
        for q in range(32):
            ve += 1; lpre_mark[q] = ve
            if q % 2 == 1:
                ve += 1; y8_mark[(q - 1) // 2] = ve
        nve_head = ve
        se = 0
        gmtev_mark = {}
        lt_mark = {}
        for g in range(8):
            se += 1; gmtev_mark[g] = se
            for qq in range(4):
                se += 1; lt_mark[g * 4 + qq] = se
        nse_head = se
        return dict(gmt=gmt_mark, cumbc=cumbc_mark, ydo=ydo_mark, npe=npe_head,
                    lpre=lpre_mark, y8=y8_mark, nve=nve_head,
                    gmtev=gmtev_mark, lt=lt_mark, nse=nse_head)

    MK = scb_marks()

    def scb_rb_xcw(eng, c):
        for p in range(8):
            i = eng.dma_start(out=scb_xcw.ap().rearrange("p (a m) -> p a m", a=64)[:, 8 * p:8 * (p + 1), :],
                              in_=xcsp.rearrange("a p m -> p a m")[:, 8 * p:8 * (p + 1), c * 128:(c + 1) * 128])
            inc(i, s_rb[0], 16)

    def scb_rb_gtw(eng, c):
        for p in range(8):
            i = eng.dma_start(out=scb_gtw.ap().rearrange("p (a m) -> p a m", a=64)[:, 8 * p:8 * (p + 1), :],
                              in_=gsp.rearrange("a p m -> p a m")[:, 8 * p:8 * (p + 1), c * 128:(c + 1) * 128])
            inc(i, s_rb[1], 16)

    def scb_rb_small(eng, c):
        i = eng.dma_start(out=scb_bcw.ap().rearrange("p (a m) -> p a m", a=16),
                          in_=xcsp.rearrange("a p m -> p a m")[:, 64:80, c * 128:(c + 1) * 128])
        inc(i, s_rg, 16)
        i = eng.dma_start(out=scb_xs[:], in_=xssp[c])
        inc(i, s_rg, 16)
        i = eng.dma_start(out=scb_plrb[:], in_=plsp[c])
        inc(i, s_rg, 16)

    def emit_scb():
        for c in range(NCH):
            with nc.Block(f"sb{c}") as blk:
                b0 = {"rb0": C(s_rb[0]), "rb1": C(s_rb[1]), "rb2": C(s_rb[2]), "pe": C(c_pe)}

                @blk.tensor
                def _(t):
                    i = t.matmul(pb[3].ap()[:, 0:128], st_onesrow[:], st_gpre.ap()[0:1, c * 128:(c + 1) * 128],
                                 start=True, stop=True)
                    inc(i, c_pe)

                @blk.vector
                def _(v):
                    if c == 0:
                        v.wait_ge(s_rb[0], C(s_rb[0]))
                    v.wait_ge(s_rg, C(s_rg))
                    if c == 0:
                        v.tensor_copy(out=scb_xcw.ap().rearrange("p (a m) -> p a m", a=64)[:, :, 0:3],
                                      in_=st_patchb.ap().rearrange("p (a q) -> p a q", a=80)[:, 0:64, :])
                        v.tensor_copy(out=scb_bcw.ap().rearrange("p (a m) -> p a m", a=16)[:, :, 0:3],
                                      in_=st_patchb.ap().rearrange("p (a q) -> p a q", a=80)[:, 64:80, :])
                    v.wait_ge(c_pe, b0["pe"] + 1)
                    v.tensor_tensor(out=scb_tmp.ap().rearrange("p (h m) -> p h m", h=128),
                                    in0=scb_pini.ap().rearrange("p (h m) -> p h m", h=128),
                                    in1=pb[3].ap()[:, 0:128].unsqueeze(2).broadcast_to([128, 128, 64]),
                                    op=OP.mult)
                    i = v.tensor_add(out=scb_plrb[:], in0=scb_plrb[:], in1=scb_tmp[:])
                    inc(i, c_ve)

            b2 = {"pe": C(c_pe), "se": C(c_se), "ve": C(c_ve), "cv": C(c_cv)}
            with nc.Block(f"sb{c}h") as blk:
                @blk.tensor
                def _(t):
                    t.wait_ge(s_rg, C(s_rg))

                    def ydo(qd):
                        q2 = (qd // 2) % 2
                        gd = qd // 4
                        if qd == 0:
                            t.wait_ge(c_ve, b2["ve"])
                        t.wait_ge(c_cv, b2["cv"] + qd + 1)
                        if qd >= 4:
                            t.wait_ge(c_ve, b2["ve"] + MK["y8"][qd // 2 - 2])
                        for hq in range(4):
                            h = qd * 4 + hq
                            t.matmul(pb[4 + q2].ap()[:, (h % 8) * 64:((h % 8) + 1) * 64],
                                     scb_gml[qd % 2][:, hq * 128:(hq + 1) * 128],
                                     scb_xs.ap()[:, h * 64:(h + 1) * 64], start=True, stop=True)
                            i = t.matmul(pb[6 + q2].ap()[:, (h % 8) * 64:((h % 8) + 1) * 64],
                                         scb_bcw.ap()[:, (8 + gd) * 128:(8 + gd + 1) * 128],
                                         scb_plrb.ap()[:, h * 64:(h + 1) * 64], start=True, stop=True)
                        inc(i, c_pe)

                    for g in range(8):
                        if g > 0:
                            t.wait_ge(c_se, b2["se"] + MK["gmtev"][g - 1])
                        i = t.matmul(pb[2].ap()[:, 0:128],
                                     scb_bcw.ap()[:, g * 128:(g + 1) * 128],
                                     scb_bcw.ap()[:, (8 + g) * 128:(8 + g + 1) * 128],
                                     start=True, stop=True)
                        inc(i, c_pe)
                        for qq in range(4):
                            q = g * 4 + qq
                            if q >= 2:
                                t.wait_ge(c_ve, b2["ve"] + MK["lpre"][q - 2])
                            for hq in range(4):
                                h = q * 4 + hq
                                i = t.matmul(pb[q % 2].ap()[:, hq * 128:(hq + 1) * 128],
                                             st_identf.ap()[:, h:h + 1].broadcast_to([128, 128]),
                                             st_cum.ap()[:, c * 128:(c + 1) * 128],
                                             start=True, stop=True)
                            inc(i, c_pe)
                            if q >= 1:
                                ydo(q - 1)
                    ydo(31)

                @blk.scalar
                def _(s):
                    for g in range(8):
                        s.wait_ge(c_pe, b2["pe"] + MK["gmt"][g])
                        i = s.copy(out=scb_gmt[g % 2][:], in_=pb[2].ap()[:, 0:128])
                        inc(i, c_se)
                        for qq in range(4):
                            q = g * 4 + qq
                            s.wait_ge(c_ve, b2["ve"] + MK["lpre"][q])
                            i = s.activation(out=scb_lt[q % 2][:], in_=scb_lpre[q % 2][:], func=AF.Exp)
                            inc(i, c_se)

                @blk.vector
                def _(v):
                    for q in range(32):
                        g = q // 4
                        v.wait_ge(c_pe, b2["pe"] + MK["cumbc"][q])
                        for hq in range(4):
                            h = q * 4 + hq
                            i = v.scalar_tensor_tensor(out=scb_lpre[q % 2][:, hq * 128:(hq + 1) * 128],
                                                       in0=pb[q % 2].ap()[:, hq * 128:(hq + 1) * 128],
                                                       scalar=st_cumT.ap()[:, c * 128 + h: c * 128 + h + 1],
                                                       in1=st_maskneg[:], op0=OP.subtract, op1=OP.add)
                        i.then_inc(c_ve, 1)
                        cnt[id(c_ve)] += 1
                        v.wait_ge(c_se, b2["se"] + MK["lt"][q])
                        i = v.tensor_tensor(out=scb_gml[q % 2].ap().rearrange("p (a m) -> p a m", a=4),
                                            in0=scb_gmt[g % 2].ap().unsqueeze(1).broadcast_to([128, 4, 128]),
                                            in1=scb_lt[q % 2].ap().rearrange("p (a m) -> p a m", a=4),
                                            op=OP.mult)
                        i.then_inc(c_cv, 1)
                        cnt[id(c_cv)] += 1
                        if q % 2 == 1:
                            bq = (q - 1) // 2
                            q2 = bq % 2
                            v.wait_ge(c_pe, b2["pe"] + MK["ydo"][bq * 2 + 1])
                            i = v.tensor_tensor(out=scb_t8.ap().rearrange("p (h m) -> p h m", h=8),
                                                in0=pb[6 + q2].ap().rearrange("p (h m) -> p h m", h=8),
                                                in1=st_sdT.ap()[:, c * 128 + bq * 8: c * 128 + bq * 8 + 8].unsqueeze(2).broadcast_to([128, 8, 64]),
                                                op=OP.mult)
                            ss(v, i, v_self)
                            i = v.tensor_tensor(out=scb_tmp.ap()[:, bq * 512:(bq + 1) * 512],
                                                in0=scb_t8[:], in1=pb[4 + q2].ap()[:], op=OP.add)
                            i.then_inc(c_ve, 1)
                            cnt[id(c_ve)] += 1
            nc.all_engine_barrier()

            b3 = {"pe": C(c_pe), "ve": C(c_ve)}
            with nc.Block(f"sb{c}y") as blk:
                @blk.tensor
                def _(t):
                    for a in range(NXT):
                        if a >= 4:
                            t.wait_ge(c_ve, b3["ve"] + a - 3)
                        i = t.transpose(pb[4 + a % 4].ap().bitcast(BF16)[:, 0:128],
                                        scb_tmp.ap()[:, a * 128:(a + 1) * 128], st_identbf[:])
                        inc(i, c_pe)

                @blk.vector
                def _(v):
                    v.wait_ge(s_rb[0], C(s_rb[0]))
                    for a in range(NXT):
                        v.wait_ge(c_pe, b3["pe"] + a + 1)
                        i = v.scalar_tensor_tensor(out=scb_ygt.ap()[:, a * 128:(a + 1) * 128],
                                                   in0=scb_xcw.ap()[:, a * 128:(a + 1) * 128],
                                                   scalar=st_drep.ap()[:, a:a + 1],
                                                   in1=pb[4 + a % 4].ap().bitcast(BF16)[:, 0:128],
                                                   op0=OP.mult, op1=OP.add)
                        i.then_inc(c_ve, 1)
                    cnt[id(c_ve)] += NXT

            b4 = {"pe": C(c_pe), "se": C(c_se), "ve": C(c_ve)}
            with nc.Block(f"sb{c}t") as blk:
                @blk.vector
                def _(v):
                    v.wait_ge(s_rb[1], C(s_rb[1]))
                    i = v.tensor_mul(out=scb_ygt[:], in0=scb_ygt[:], in1=scb_gtw[:])
                    inc(i, c_ve)

                @blk.scalar
                def _(s):
                    s.wait_ge(c_ve, b4["ve"] + 1)
                    i = s.activation(out=scb_tmp[:], in_=scb_ygt[:], func=AF.Square)
                    inc(i, c_se)
                    s.wait_ge(c_pe, b4["pe"] + 1)
                    s.copy(out=st_gnr.ap()[0:1, 0:512], in_=pb[2].ap()[0:1, :])
                    i = s.copy(out=st_gnr.ap()[0:1, 512:1024], in_=pb[3].ap()[0:1, :])
                    ss(s, i, s_self)
                    i = s.activation(out=st_gnr[:], in_=st_gnr[:], func=AF.Ln, scale=1.0 / 1024.0, bias=st_konst.ap()[0:1, 1:2])
                    ss(s, i, s_self)
                    i = s.activation(out=st_gnr[:], in_=st_gnr[:], func=AF.Exp, scale=-0.5)
                    inc(i, c_se)

                @blk.tensor
                def _(t):
                    t.wait_ge(c_se, b4["se"] + 1)
                    for g in range(8):
                        bank = pb[2 + g // 4]
                        for kk in range(8):
                            i = t.matmul(bank.ap()[0:1, (g % 4) * 128:(g % 4 + 1) * 128],
                                         st_onesbf[:], scb_tmp.ap()[:, (g * 8 + kk) * 128:(g * 8 + kk + 1) * 128],
                                         start=(kk == 0), stop=(kk == 7))
                    inc(i, c_pe)
                    t.wait_ge(c_se, b4["se"] + 2)
                    for g in range(8):
                        if g >= 2:
                            t.wait_ge(c_ve, b4["ve"] + 1 + g - 1)
                        i = t.matmul(pb[4 + g % 2].ap()[:, 0:128], st_onesrow[:],
                                     st_gnr.ap()[0:1, g * 128:(g + 1) * 128], start=True, stop=True)
                        inc(i, c_pe)

                @blk.vector
                def _(v2):
                    for g in range(8):
                        v2.wait_ge(c_pe, b4["pe"] + 1 + g + 1)
                        for aa in range(8):
                            a = g * 8 + aa
                            i = v2.scalar_tensor_tensor(out=scb_ygt.ap()[:, a * 128:(a + 1) * 128],
                                                        in0=scb_ygt.ap()[:, a * 128:(a + 1) * 128],
                                                        scalar=st_nw.ap()[:, a:a + 1],
                                                        in1=pb[4 + g % 2].ap()[:, 0:128],
                                                        op0=OP.mult, op1=OP.mult)
                        i.then_inc(c_ve, 1)
                        cnt[id(c_ve)] += 1

                @blk.sync
                def _(sy):
                    if c + 1 < NCH:
                        sy.wait_ge(c_ve, b4["ve"] + 1)
                        scb_rb_gtw(sy, c + 1)
                        scb_rb_xcw(sy, c + 1)
                    sy.wait_ge(c_ve, C(c_ve))
                    for p in range(8):
                        i = sy.dma_start(out=ysp.rearrange("a p m -> p a m")[:, 8 * p:8 * (p + 1), c * 128:(c + 1) * 128],
                                         in_=scb_ygt.ap().rearrange("p (a m) -> p a m", a=64)[:, 8 * p:8 * (p + 1), :])
                        inc(i, s_rb[3], 16)
                    sy.wait_ge(s_rb[3], C(s_rb[3]))

                @blk.gpsimd
                def _(g):
                    if c + 1 < NCH:
                        g.wait_ge(c_pe, b4["pe"])
                        scb_rb_small(g, c + 1)
            nc.all_engine_barrier()

    def emit_op(half):
        NS = 16
        opw3 = opwT.rearrange("(a p) m -> p a m", p=128)
        b_sd, b_se, b_ve = C(c_sd), C(c_se), C(c_ve)
        b_wl = [C(s) for s in s_wl]
        b_xl = [C(s) for s in s_xl]
        b_sp = [C(s) for s in s_sp]
        b_rb = C(s_rb[0])
        kbase = half * 32

        with nc.Block(f"op{half}") as blk:
            @blk.sync
            def _(sy0):
                for a in range(32):
                    i = sy0.dma_start(out=xln.ap()[:, a * 1024:(a + 1) * 1024], in_=ysp[kbase + a])
                    inc(i, s_rb[0], 16)

            @blk.gpsimd
            def _(g):
                def wl_static(u, par):
                    for p in range(4):
                        g.dma_start(out=ws[par].ap().rearrange("p (a m) -> p a m", a=32)[:, 8 * p:8 * (p + 1), :],
                                    in_=opw3[:, 8 * p + kbase:8 * (p + 1) + kbase, u * 256:u * 256 + 256]).then_inc(s_wl[par], 16)
                    cnt[id(s_wl[par])] += 64
                for u in range(2):
                    wl_static(u, u)
                for u in range(2, NS):
                    g.wait_ge(c_sd, b_sd + u - 1)
                    wl_static(u, u % 2)

            @blk.tensor
            def _(t):
                gate = c_se if half == 0 else c_ve
                b_gate = b_se if half == 0 else b_ve
                bump(t, gate, 4)
                t.wait_ge(s_rb[0], b_rb + 16 * 32)
                with t.register(f"top{half}") as r:
                    with t.Fori(0, NS // 2) as s2:
                        for par in range(2):
                            aw(t, r, (s2, 64), b_wl[par] + 64, s_wl[par])
                            aw(t, r, (s2, 8), b_gate + 4 + 4 * par - 4, gate)
                            for j in range(2):
                                for k in range(32):
                                    for ts in range(2):
                                        mm = t.matmul(pb[par * 4 + j * 2 + ts][:],
                                                      ws[par][:, k * 256 + j * 128: k * 256 + j * 128 + 128],
                                                      xln[:, k * 1024 + ts * 512: k * 1024 + ts * 512 + 512],
                                                      start=(k == 0), stop=(k == 31))
                            mm.then_inc(c_sd, 1)
                cnt[id(c_sd)] += NS

            if half == 0:
                @blk.scalar
                def _(s):
                    bump(s, s_sp[0], 32)
                    bump(s, s_sp[1], 32)
                    with s.register("sop0") as r:
                        with s.Fori(0, NS // 2) as s2:
                            for par in range(2):
                                aw(s, r, (s2, 2), b_sd + par + 1, c_sd)
                                aw(s, r, (s2, 64), b_sp[1 - par] + 64 * par, s_sp[1 - par])
                                for j in range(2):
                                    for ts in range(2):
                                        q = j * 2 + ts
                                        i = s.activation(out=ops_ps(par * 4 + q),
                                                         in_=pb[par * 4 + j * 2 + ts].ap()[:], func=AF.Copy)
                                        i.then_inc(c_se, 1)
                    cnt[id(c_se)] += NS * 4

                @blk.sync
                def _(sy):
                    for u in range(NS):
                        par = u % 2
                        for j in range(2):
                            for ts in range(2):
                                q = j * 2 + ts
                                sy.wait_ge(c_se, b_se + 4 + 4 * u + q + 1)
                                sy.dma_start(out=psp[u * 2 + j][:, ts * 512:(ts + 1) * 512],
                                             in_=ops_ps(par * 4 + q)).then_inc(s_sp[par], 16)
                        cnt[id(s_sp[par])] += 64
                    sy.wait_ge(s_sp[0], C(s_sp[0]))
                    sy.wait_ge(s_sp[1], C(s_sp[1]))
            else:
                @blk.sync
                def _(sy):
                    def ldpair(u, par):
                        for j in range(2):
                            for ts in range(2):
                                q = j * 2 + ts
                                sy.dma_start(out=ops_ps(par * 4 + q),
                                             in_=psp[u * 2 + j][:, ts * 512:(ts + 1) * 512]).then_inc(s_xl[par], 16)
                                sy.dma_start(out=ops_x1(par * 4 + q),
                                             in_=x1sp[u * 2 + j][:, ts * 512:(ts + 1) * 512]).then_inc(s_xl[par], 16)
                        cnt[id(s_xl[par])] += 128

                    for u in range(2):
                        ldpair(u, u)
                    for u in range(NS):
                        par = u % 2
                        for j in range(2):
                            for ts in range(2):
                                sy.wait_ge(c_ve, b_ve + 4 + 4 * u + j * 2 + ts + 1)
                                sy.dma_start(out=x2sp[u * 2 + j][:, ts * 512:(ts + 1) * 512],
                                             in_=ops_ps(par * 4 + j * 2 + ts)).then_inc(s_sp[par], 16)
                        cnt[id(s_sp[par])] += 64
                        if u + 2 < NS:
                            ldpair(u + 2, par)
                    sy.wait_ge(s_sp[0], C(s_sp[0]))
                    sy.wait_ge(s_sp[1], C(s_sp[1]))

                @blk.vector
                def _(v):
                    with v.register("vop1") as r:
                        with v.Fori(0, NS // 2) as s2:
                            for par in range(2):
                                aw(v, r, (s2, 2), b_sd + par + 1, c_sd)
                                for j in range(2):
                                    for ts in range(2):
                                        q = j * 2 + ts
                                        aw(v, r, (s2, 128), b_xl[par] + 32 * q + 32, s_xl[par])
                                        i = v.tensor_add(out=ops_ps(par * 4 + q), in0=ops_ps(par * 4 + q),
                                                         in1=pb[par * 4 + q].ap()[:])
                                        ss(v, i, v_self)
                                        i = v.tensor_add(out=ops_ps(par * 4 + q), in0=ops_ps(par * 4 + q),
                                                         in1=ops_x1(par * 4 + q))
                                        i.then_inc(c_ve, 1)
                    cnt[id(c_ve)] += NS * 4
        nc.all_engine_barrier()

    import os as _os
    _cut = _os.environ.get("K_CUT", "")
    ipw3 = ipwT.rearrange("(a p) m -> p a m", p=128)

    def pf_ipg():
        base = [C(s) for s in s_wl]
        with nc.Block("pfipg") as blk:
            @blk.gpsimd
            def _(g):
                for u in range(2):
                    for p in range(4):
                        g.dma_start(out=ws[u].ap().rearrange("p (a m) -> p a m", a=32)[:, 8 * p:8 * (p + 1), :],
                                    in_=ipw3[:, 8 * p:8 * (p + 1), u * 256:u * 256 + 256]).then_inc(s_wl[u], 16)
                    cnt[id(s_wl[u])] += 64
        return base

    def pf_ffn():
        base = [C(s) for s in s_wl]
        wre = wbT.rearrange("(a p) m -> p a m", p=128)
        with nc.Block("pfffn") as blk:
            @blk.gpsimd
            def _(g):
                for u in range(2):
                    for p in range(4):
                        g.dma_start(out=ws[u].ap().rearrange("p (a m) -> p a m", a=32)[:, 8 * p:8 * (p + 1), :],
                                    in_=wre[:, 8 * p:8 * (p + 1), u * 256:u * 256 + 256]).then_inc(s_wl[u], 16)
                    g.dma_start(out=stg_ball[u][:], in_=ballT[:, u * 256:u * 256 + 256]).then_inc(s_wl[u], 16)
                    cnt[id(s_wl[u])] += 80
        return base

    def _phases():
        yield "ln1"; emit_ln("ln1", xT3, sb["xln"])
        yield "aux1"; emit_aux("f1")
        yield "ffn1"; emit_ffn("f1", wbT, xT3, x1sp)
        yield "ln2"; _pre_ipg = pf_ipg(); emit_ln("ln2", x1sp, sb["xln"])
        yield "ipg"; emit_ip_gate(pre=_pre_ipg)
        yield "ipx"; emit_ip_xbc()
        yield "dt"; emit_dt()
        yield "ctx"; emit_ctx()
        yield "sca"; emit_sca()
        yield "pex"; emit_pex()
        yield "scb"; emit_scb()
        yield "op0"; emit_op(0)
        yield "op1"; emit_op(1)
        yield "ln3"; _pre_f2 = pf_ffn(); emit_ln("ln3", x2sp, sb["xln"])
        yield "aux2"; emit_aux("f2", reload_a=True)
        yield "ffn2"; emit_ffn("f2", wbT, x2sp, out3, pre=_pre_f2)

    _g = _phases()
    for _name in _g:
        if _cut and _name == _cut:
            break
    es.close()
    nc._sb = sb
    return nc


_NC_CACHE = {}


def _prep(inputs):
    f32 = np.float32
    x = np.asarray(inputs["L0_attn_hidden_states"], f32)
    ipw = np.asarray(inputs["L0_attn_in_proj_w"], f32)
    cw = np.asarray(inputs["L0_attn_conv_w"], f32).reshape(CONV, 4)
    cb = np.asarray(inputs["L0_attn_conv_b"], f32)
    dtb = np.asarray(inputs["L0_attn_dt_bias"], f32)
    alog = np.asarray(inputs["L0_attn_A_log"], f32)
    D = np.asarray(inputs["L0_attn_D"], f32)
    nw = np.asarray(inputs["L0_attn_norm_w"], f32)
    opw = np.asarray(inputs["L0_attn_out_proj_w"], f32)
    wb = np.asarray(inputs["L0_ffn_W_base"], f32)
    bb = np.asarray(inputs["L0_ffn_b_base"], f32)
    la = np.asarray(inputs["L0_ffn_lora_A"], f32)
    lb = np.asarray(inputs["L0_ffn_lora_B"], f32)
    rw = np.asarray(inputs["L0_ffn_router_W"], f32)
    rb = np.asarray(inputs["L0_ffn_router_b"], f32)
    dp = np.asarray(inputs["L0_ffn_domain_prior"], f32)
    act = np.asarray(inputs["L0_ffn_active"], np.int32)

    C_ = np.ascontiguousarray
    shared = {
        "wbT": C_(wb.T),
        "ipwT": C_(ipw.T),
        "opwT": C_(opw.T),
        "aallT": C_(la.transpose(2, 0, 1).reshape(HID, 128)),
        "ballT": C_(lb.transpose(0, 2, 1).reshape(128, HID)),
        "rwT": C_(rw.T),
        "bbcol": C_(bb.reshape(HT, 128).T),
        "prior_c": C_(dp.reshape(16, 1)),
        "rb_c": C_(rb.reshape(16, 1)),
        "iota16": np.arange(16, dtype=f32).reshape(1, 16),
        "activ": C_(act.reshape(1, 4)),
        "rexp": (lambda m: m)(np.zeros((16, 128), f32)),
        "identbf": np.eye(128).astype(BF),
        "identf": np.eye(128, dtype=f32),
        "onesbf": np.ones((128, 1), BF),
        "onesrow": np.ones((1, 128), f32),
        "ones16": np.ones((16, 1), f32),
        "masknegf": np.where(np.arange(128)[None, :] >= np.arange(128)[:, None], 0.0, -1e30).astype(f32),
        "cwT": C_(cw.reshape(80, 128, 4).transpose(1, 0, 2).reshape(128, 320)),
        "cbT": C_(cb.reshape(80, 128).T),
        "dtb_c": C_(dtb.reshape(128, 1)),
        "alog_c": C_(alog.reshape(128, 1)),
        "drep": C_(np.repeat(D, 64).reshape(NXT, 128).T),
        "nwT": C_(nw.reshape(NXT, 128).T),
    }
    for e in range(16):
        shared["rexp"][e, e * 8:(e + 1) * 8] = 2.0
    in_maps = []
    for core in range(8):
        b, h = core // 2, core % 2
        m = dict(shared)
        m["xT"] = C_(x[b, h * T:(h + 1) * T, :].T)
        m["cmask"] = np.full((128, 1), 1.0 if h == 1 else 0.0, f32)
        in_maps.append(m)
    return in_maps


def kernel(**inputs):
    in_maps = _prep(inputs)
    if "nc" not in _NC_CACHE:
        _NC_CACHE["nc"] = build()
    nc = _NC_CACHE["nc"]
    res = run_bass_kernel_spmd(nc, in_maps, core_ids=list(range(8)))
    x = np.asarray(inputs["L0_attn_hidden_states"])
    out = np.empty((4, 2048, HID), np.float32)
    for core in range(8):
        b, h = core // 2, core % 2
        out[b, h * T:(h + 1) * T, :] = res.results[core]["outT"].T
    return out


# version 09
# speedup vs baseline: 1.0296x; 1.0296x over previous
import contextlib
import numpy as np
import ml_dtypes

import concourse.bass as bass
import concourse.mybir as mybir
from concourse.bass_utils import run_bass_kernel_spmd

F32 = mybir.dt.float32
BF16 = mybir.dt.bfloat16
I32 = mybir.dt.int32
AF = mybir.ActivationFunctionType
OP = mybir.AluOpType
BF = ml_dtypes.bfloat16

HID = 4096
HT = 32
DIN = 8192
G = 8
CONV = 10240
PROJ = 18560
T = 1024
CS = 128
NCH = 8
EPS = 1e-5
NXT = 64
LN_INV = 1.0 / HID


def build(dbg=False, races=True):
    import os as _os0
    _GPV = bool(int(_os0.environ.get("K_GPV", "0")))
    nc = bass.Bass("TRN2", target_bir_lowering=False, detect_race_conditions=races)
    es = contextlib.ExitStack()

    def din(name, shape, dt=F32):
        return nc.dram_tensor(name, shape, dt, kind="ExternalInput").ap()

    xT = din("xT", [HID, T])
    wbT = din("wbT", [HID, HID])
    ipwT = din("ipwT", [HID, PROJ])
    opwT = din("opwT", [DIN, HID])
    aallT = din("aallT", [HID, 128])
    ballT = din("ballT", [128, HID])
    rwT = din("rwT", [HID, 16])
    rexp = din("rexp", [16, 128])
    bbcol = din("bbcol", [128, HT])
    prior_c = din("prior_c", [16, 1])
    rb_c = din("rb_c", [16, 1])
    iota16 = din("iota16", [1, 16])
    activ = din("activ", [1, 4], I32)
    identbf = din("identbf", [128, 128], BF16)
    identf = din("identf", [128, 128])
    onesbf = din("onesbf", [128, 1], BF16)
    onesrow = din("onesrow", [1, 128])
    ones16 = din("ones16", [16, 1])
    masknegf = din("masknegf", [128, 128])
    cwT = din("cwT", [128, 320])
    cbT = din("cbT", [128, 80])
    dtb_c = din("dtb_c", [128, 1])
    alog_c = din("alog_c", [128, 1])
    drep = din("drep", [128, NXT])
    nwT = din("nwT", [128, NXT])
    cmask = din("cmask", [128, 1])

    kindD = "ExternalOutput" if dbg else "Internal"
    x1sp = nc.dram_tensor("x1sp", [HT, 128, T], F32, kind=kindD).ap()
    x2sp = nc.dram_tensor("x2sp", [HT, 128, T], F32, kind=kindD).ap()
    gsp = nc.dram_tensor("gsp", [64, 128, T], BF16, kind=kindD).ap()
    xcsp = nc.dram_tensor("xcsp", [80, 128, T], BF16, kind=kindD).ap()
    xssp = nc.dram_tensor("xssp", [NCH, 128, DIN], BF16, kind=kindD).ap()
    plsp = nc.dram_tensor("plsp", [NCH, 128, DIN], BF16, kind=kindD).ap()
    psp = nc.dram_tensor("psp", [HT, 128, T], F32, kind="Internal").ap()
    ysp = nc.dram_tensor("ysp", [64, 128, T], BF16, kind=kindD).ap()
    ctxin = nc.dram_tensor("ctxin", [128, 240], BF16, kind="Internal")
    raw03_d = nc.dram_tensor("raw03_d", [128, 240], BF16, kind="Internal")
    bbh_d = nc.dram_tensor("bbh_d", [128, HT], F32, kind="Internal")
    ctxout = nc.dram_tensor("ctxout", [256, 240], BF16, kind="Internal")
    pstin = nc.dram_tensor("pstin", [128, DIN], F32, kind="Internal")
    pstout = nc.dram_tensor("pstout", [256, DIN], F32, kind="Internal")
    outT = nc.dram_tensor("outT", [HID, T], F32, kind="ExternalOutput").ap()

    xT3 = xT.rearrange("(a p) m -> a p m", p=128)
    out3 = outT.rearrange("(a p) m -> a p m", p=128)

    sb = {}
    _stat_off = [16512]
    _stat_pos = {}

    def stat(name, shape, dt):
        nbytes = int(np.prod(shape[1:])) * (2 if dt in (BF16,) else 4)
        off = (_stat_off[0] + 63) // 64 * 64
        _stat_off[0] = off + nbytes
        assert _stat_off[0] <= 65408, (name, _stat_off[0])
        _stat_pos[name] = off
        h = nc.alloc_sbuf_tensor_at(name, shape, dt, offset=off)
        sb[name] = h
        return h

    def at(name, shape, dt, off):
        h = nc.alloc_sbuf_tensor_at(name, shape, dt, offset=off)
        sb[name] = h
        return h

    st_identbf = stat("st_identbf", [128, 128], BF16)
    st_identf = stat("st_identf", [128, 128], F32)
    st_maskneg = stat("st_maskneg", [128, 128], F32)
    st_onesbf = stat("st_onesbf", [128, 1], BF16)
    st_onesrow = stat("st_onesrow", [1, 128], F32)
    st_ones16 = stat("st_ones16", [16, 1], F32)
    st_o16r = stat("st_o16r", [1, 16], F32)
    st_iota = stat("st_iota", [1, 16], F32)
    st_activf = stat("st_activf", [1, 4], F32)
    st_mrow = stat("st_mrow", [1, 16], F32)
    st_mcol = stat("st_mcol", [16, 1], F32)
    st_bias16 = stat("st_bias16", [16, 1], F32)
    st_scr16 = stat("st_scr16", [16, 1], F32)
    st_bbh = stat("st_bbh", [128, HT], F32)
    st_rexp = stat("st_rexp", [16, 128], F32)
    st_rw = stat("st_rw", [128, HT * 16], BF16)
    st_ball = stat("st_ball", [128, HID], BF16)
    st_cw = stat("st_cw", [128, 320], F32)
    st_cb = stat("st_cb", [128, 80], F32)
    st_dtb = stat("st_dtb", [128, 1], F32)
    st_alog = stat("st_alog", [128, 1], F32)
    st_acol = stat("st_acol", [128, 1], F32)
    st_drep = stat("st_drep", [128, NXT], F32)
    st_nw = stat("st_nw", [128, NXT], F32)
    st_cmask = stat("st_cmask", [128, 1], F32)
    st_dtv = stat("st_dtv", [128, T], F32)
    st_cum = stat("st_cum", [128, T], F32)
    st_cumT = stat("st_cumT", [128, T], F32)
    st_sdT = None
    st_dtok = None
    st_dtvT = None
    st_grow = stat("st_grow", [1, NCH * 128], F32)
    st_gpre = stat("st_gpre", [1, NCH * 128], F32)
    st_cend = stat("st_cend", [1, 256], F32)
    st_gnr = stat("st_gnr", [1, T], F32)
    st_presil = stat("st_presil", [128, 240], F32)
    st_ctxs = stat("st_ctxs", [128, 240], BF16)
    st_ctxr = stat("st_ctxr", [128, 240], BF16)
    st_patch = stat("st_patch", [128, 240], F32)
    st_patch2 = stat("st_patch2", [128, 240], F32)
    st_patchb = stat("st_patchb", [128, 240], BF16)
    st_raw03 = stat("st_raw03", [128, 240], BF16)
    st_konst = stat("st_konst", [128, 4], F32)

    A0, A1, AW = 65408, 130944, 196480

    xln = at("xln", [128, HT * 1024], BF16, A0)
    sca_xcw = [at(f"sca_xcw{i}", [128, 64 * 128], BF16, A0 + i * 16384) for i in range(2)]
    sca_xs = at("sca_xs", [128, DIN], BF16, A0 + 32768)
    sca_xs2 = at("sca_xs2", [128, DIN], BF16, A0 + 49152)
    sca_btok = at("sca_btok", [128, 8 * 128], BF16, AW)
    scb_gtw = at("scb_gtw", [128, DIN], BF16, A0)
    scb_xs = at("scb_xs", [128, DIN], BF16, A0 + 16384)
    scb_plrb = at("scb_plrb", [128, DIN], BF16, A0 + 32768)
    scb_ygt = at("scb_ygt", [128, DIN], BF16, A0 + 49152)

    a1_ball_src = at("a1_aall", [128, HT * 128], BF16, A1)
    st_zs = at("st_zs", [128, T], BF16, A1 + 8192)
    st_alpha = at("st_alpha", [16, T], F32, A1 + 10240)
    st_pmask = at("st_pmask", [16, T], F32, A1 + 14336)
    st_dtraw = at("st_dtraw", [128, T], F32, A1 + 18432)
    st_row = at("st_row", [1, T], F32, A1 + 22528)
    st_row2 = at("st_row2", [1, T], F32, A1 + 26624)
    st_rowt = at("st_rowt", [1, T], F32, A1 + 30720)
    stg_ev = [at(f"stg_ev{i}", [128, 512], F32, A1 + 34816 + i * 2048) for i in range(4)]
    xrow = [at(f"xrow{i}", [128, 1030], BF16, A1 + 43008 + i * 2112) for i in range(2)]
    diag4 = at("diag4", [128, 2 * 512], BF16, A1 + 47232)
    gstage = [at(f"gstage{i}", [128, 1024], BF16, A1 + 49280 + i * 2048) for i in range(2)]
    stg_bb = [at(f"stg_bb{i}", [128, 2], F32, A1 + 53376 + i * 64) for i in range(2)]
    stg_ball = [at(f"stg_ball{i}", [128, 256], BF16, A1 + 53504 + i * 512) for i in range(2)]
    stg_cw = [at(f"stg_cw{i}", [128, 8], F32, A1 + 54528 + i * 64) for i in range(2)]
    stg_cb = [at(f"stg_cb{i}", [128, 2], F32, A1 + 54656 + i * 64) for i in range(2)]
    stg_smb = at("stg_smb", [128, 12], BF16, A1 + 54784)
    stg_sm6 = at("stg_sm6", [128, 12], BF16, A1 + 54848)
    stg_x = [at(f"stg_x{i}", [128, 1024], BF16, A1 + 55296 + i * 2048) for i in range(4)]
    stg_sq = [at(f"stg_sq{i}", [128, 1024], BF16, A1 + 63488 + i * 2048) for i in range(1)]
    scb_xcw = at("scb_xcw", [128, 64 * 128], BF16, A1)
    scb_pini = at("scb_pini", [128, DIN], BF16, A1 + 16384)
    scb_tmp = at("scb_tmp", [128, DIN], BF16, A1 + 32768)
    scb_t8 = at("scb_t8", [128, 512], F32, AW + 8192)
    scb_gmt = [at(f"scb_gmt{i}", [128, 128], BF16, AW + 10240 + i * 256) for i in range(2)]
    scb_lpre = [at(f"scb_lpre{i}", [128, 512], F32, AW + 10752 + i * 2048) for i in range(2)]
    scb_lt = [at(f"scb_lt{i}", [128, 512], BF16, AW + 14848 + i * 1024) for i in range(2)]
    scb_gml = [at(f"scb_gml{i}", [128, 512], BF16, AW + 16896 + i * 1024) for i in range(2)]
    sca_p = [at(f"sca_p{i}", [128, DIN], F32, A1 + i * 32768) for i in range(2)]

    ws = [at(f"ws{i}", [128, 32 * 256], BF16, AW + i * 16384) for i in range(2)]
    st_sdT = at("st_sdT", [128, T], F32, AW + 19456)
    st_dtok = at("st_dtok", [128, T], F32, AW + 23552)
    st_dtvT = at("st_dtvT", [128, T], BF16, AW + 27648)
    st_dtv2T = at("st_dtv2T", [128, T], BF16, AW + 29696)
    st_gamT = at("st_gamT", [128, T], F32, _stat_pos["st_dtv"])
    scb_bcw = at("scb_bcw", [128, 16 * 128], BF16, AW)
    sca_bw = [at(f"sca_bw{i}", [128, 8 * 128], BF16, AW + 4096 + i * 2048) for i in range(2)]
    assert AW + 32768 <= 229376

    psL = es.enter_context(nc.psum_tensor("psL", [128, 2048], F32))
    psH = es.enter_context(nc.psum_tensor("psH", [128, 2048], F32))

    class _PB:
        __slots__ = ("h", "k")

        def __init__(self, h, k):
            self.h = h
            self.k = k

        def ap(self):
            return self.h.ap()[:, self.k * 512:(self.k + 1) * 512]

        def __getitem__(self, sl):
            return self.ap()[sl]

    pb = [_PB(psL, i) for i in range(4)] + [_PB(psH, i) for i in range(4)]

    sems = []

    def mksem(name):
        h = es.enter_context(nc.semaphore(name))
        sems.append(h)
        return h

    s_cst = [mksem(f"s_cst{i}") for i in range(4)]
    s_wl = [mksem(f"s_wl{i}") for i in range(2)]
    s_xl = [mksem(f"s_xl{i}") for i in range(2)]
    s_sp = [mksem(f"s_sp{i}") for i in range(2)]
    s_rb = [mksem(f"s_rb{i}") for i in range(4)]
    s_rc = [mksem(f"s_rc{i}") for i in range(2)]
    s_rg = mksem("s_rg")
    s_st = [mksem(f"s_st{i}") for i in range(2)]
    s_cc = mksem("s_cc")
    c_sd = mksem("c_sd")
    c_ev = mksem("c_ev")
    c_se = mksem("c_se")
    c_ve = mksem("c_ve")
    c_ln = mksem("c_ln")
    c_pe = mksem("c_pe")
    c_cv = mksem("c_cv")
    v_self = mksem("v_self")
    s_self = mksem("s_self")

    cnt = {id(h): 0 for h in sems}

    def inc(inst, s, k=1):
        inst.then_inc(s, k)
        cnt[id(s)] += k

    def bump(eng, s, k=1):
        eng.sem_inc(s, k)
        cnt[id(s)] += k

    def C(s):
        return cnt[id(s)]

    def ss(eng, i, sem):
        inc(i, sem, 1)
        eng.wait_ge(sem, C(sem))

    def aw(eng, r, mul_s2, addc, sem):
        assert addc >= 0, addc
        eng.reg_mov(r, mul_s2[0])
        eng.reg_mul(r, r, mul_s2[1])
        if addc:
            eng.reg_add(r, r, addc)
        eng.wait_ge(sem, r)

    with nc.Block("pk") as blk:
        @blk.vector
        def _(v):
            v.memset(st_konst.ap()[:, 0:1], 1e-12)
            v.memset(st_konst.ap()[:, 1:2], EPS)
            i = v.memset(st_konst.ap()[:, 2:3], 1.0)
            inc(i, c_ve)
    with nc.Block("p0") as blk:
        @blk.sync
        def _(sy):
            loads = [(st_identbf, identbf), (st_identf, identf), (st_maskneg, masknegf),
                     (st_onesbf, onesbf), (st_onesrow, onesrow), (st_ones16, ones16),
                     (st_iota, iota16), (st_rexp, rexp), (st_cw, cwT), (st_cb, cbT),
                     (st_dtb, dtb_c), (st_alog, alog_c), (st_drep, drep), (st_nw, nwT),
                     (st_cmask, cmask), (st_scr16, prior_c), (st_mcol, rb_c), (st_bbh, bbcol),
                     (st_o16r, iota16)]
            for d, s in loads:
                i = sy.dma_start(out=d[:], in_=s)
                inc(i, s_xl[0], 16)

        @blk.gpsimd
        def _(g):
            i = g.dma_start(out=st_activf[:], in_=activ)
            inc(i, s_cst[0], 16)
            i = g.dma_start(out=st_rw.ap().rearrange("p (a m) -> p a m", a=HT),
                            in_=rwT.rearrange("(a p) m -> p a m", p=128))
            inc(i, s_cst[0], 16)
            for h in range(2):
                i = g.dma_start(out=a1_ball_src.ap().rearrange("p (a m) -> p a m", a=HT)[:, 16 * h:16 * (h + 1), :],
                                in_=aallT.rearrange("(a p) m -> p a m", p=128)[:, 16 * h:16 * (h + 1), :])
                inc(i, s_cst[0], 16)
            i = g.dma_start(out=st_ball[:], in_=ballT)
            inc(i, s_cst[0], 16)

        @blk.scalar
        def _(s):
            s.wait_ge(s_xl[0], C(s_xl[0]))
            s.wait_ge(s_cst[0], C(s_cst[0]))
            s.activation(out=st_acol[:], in_=st_alog[:], func=AF.Exp)
            s.wait_ge(c_ve, 1)
            s.activation(out=st_bias16[:], in_=st_scr16[:], func=AF.Ln, bias=st_konst.ap()[0:16, 0:1])
            i = s.mul(st_bbh[:], st_bbh[:], 0.5)
            inc(i, c_se)

        @blk.vector
        def _(v):
            v.wait_ge(c_se, C(c_se))
            v.tensor_add(out=st_bias16[:], in0=st_bias16[:], in1=st_mcol[:])
            i = v.memset(st_mrow[:], 0.0)
            ss(v, i, v_self)
            for j in range(4):
                i = v.tensor_scalar(out=st_o16r[:], in0=st_iota[:], scalar1=st_activf[0:1, j:j + 1],
                                    scalar2=None, op0=OP.is_equal)
                ss(v, i, v_self)
                i = v.tensor_add(out=st_mrow[:], in0=st_mrow[:], in1=st_o16r[:])
                if j < 3:
                    ss(v, i, v_self)
            inc(i, c_ve)

        @blk.tensor
        def _(t):
            t.wait_ge(c_ve, C(c_ve))
            i = t.transpose(pb[0].ap()[0:16, 0:1], st_mrow[:], st_identf.ap()[0:1, 0:1])
            inc(i, c_pe)

    with nc.Block("p0b") as blk:
        @blk.vector
        def _(v):
            v.wait_ge(c_pe, C(c_pe))
            v.tensor_copy(out=st_mcol[:], in_=pb[0].ap()[0:16, 0:1])
            v.memset(st_o16r[:], 1.0)

        @blk.sync
        def _(sy):
            sy.wait_ge(c_se, C(c_se))
            i = sy.dma_start(out=bbh_d[:, :], in_=st_bbh[:])
            inc(i, s_rb[0], 16)
            sy.wait_ge(s_rb[0], C(s_rb[0]))
    nc.all_engine_barrier()

    def emit_ln(tag, src3, dstbuf):
        b_cst = [C(s) for s in s_cst]
        b_ln, b_se, b_pe, b_ve = C(c_ln), C(c_se), C(c_pe), C(c_ve)
        with nc.Block(f"ln_{tag}") as blk:
            @blk.gpsimd
            def _(g):
                for p in range(2):
                    for a in range(HT):
                        sl = a % 4
                        n = p * HT + a
                        if n >= 4:
                            g.wait_ge(c_ln, b_ln + n - 3)
                        i = g.dma_start(out=stg_x[sl][:], in_=src3[a])
                        inc(i, s_cst[sl], 16)

            @blk.scalar
            def _(s):
                for a in range(HT):
                    sl = a % 4
                    s.wait_ge(s_cst[sl], b_cst[sl] + 16 * (a // 4 + 1))
                    if a >= 1:
                        s.wait_ge(c_ln, b_ln + a)
                    i = s.activation(out=stg_sq[0][:], in_=stg_x[sl][:], func=AF.Square)
                    inc(i, c_se)
                s.wait_ge(c_ve, b_ve + 1)
                i = s.activation(out=st_row2[:], in_=st_row2[:], func=AF.Ln, bias=st_konst.ap()[0:1, 1:2])
                ss(s, i, s_self)
                i = s.activation(out=st_row2[:], in_=st_row2[:], func=AF.Exp, scale=-0.5)
                inc(i, c_se)

            @blk.tensor
            def _(t):
                for a in range(HT):
                    sl = a % 4
                    t.wait_ge(s_cst[sl], b_cst[sl] + 16 * (a // 4 + 1))
                    t.wait_ge(c_se, b_se + a + 1)
                    for ts in range(2):
                        t.matmul(pb[ts].ap()[0:1, :], st_onesbf[:], stg_x[sl][:, ts * 512:(ts + 1) * 512],
                                 start=(a == 0), stop=(a == HT - 1))
                    for ts in range(2):
                        i = t.matmul(pb[2 + ts].ap()[0:1, :], st_onesbf[:], stg_sq[0][:, ts * 512:(ts + 1) * 512],
                                     start=(a == 0), stop=(a == HT - 1))
                    inc(i, c_ln)
                t.wait_ge(c_se, b_se + HT + 1)
                for ts in range(2):
                    t.matmul(pb[4 + ts][:], st_onesrow[:], st_row.ap()[0:1, ts * 512:(ts + 1) * 512],
                             start=True, stop=True)
                for ts in range(2):
                    i = t.matmul(pb[6 + ts][:], st_onesrow[:], st_row2.ap()[0:1, ts * 512:(ts + 1) * 512],
                                 start=True, stop=True)
                inc(i, c_pe)

            @blk.vector
            def _(v):
                v.wait_ge(c_ln, b_ln + HT)
                v.tensor_scalar(out=st_row.ap()[0:1, 0:512], in0=pb[0].ap()[0:1, :], scalar1=LN_INV, scalar2=None, op0=OP.mult)
                v.tensor_scalar(out=st_row.ap()[0:1, 512:1024], in0=pb[1].ap()[0:1, :], scalar1=LN_INV, scalar2=None, op0=OP.mult)
                v.tensor_scalar(out=st_row2.ap()[0:1, 0:512], in0=pb[2].ap()[0:1, :], scalar1=LN_INV, scalar2=None, op0=OP.mult)
                i = v.tensor_scalar(out=st_row2.ap()[0:1, 512:1024], in0=pb[3].ap()[0:1, :], scalar1=LN_INV, scalar2=None, op0=OP.mult)
                ss(v, i, v_self)
                i = v.tensor_mul(out=st_rowt[:], in0=st_row[:], in1=st_row[:])
                ss(v, i, v_self)
                i = v.tensor_sub(out=st_row2[:], in0=st_row2[:], in1=st_rowt[:])
                inc(i, c_ve)
                v.wait_ge(c_pe, b_pe + 1)
                for a in range(HT):
                    sl = a % 4
                    n = HT + a
                    v.wait_ge(s_cst[sl], b_cst[sl] + 16 * (n // 4 + 1))
                    for ts in range(2):
                        i = v.tensor_sub(out=stg_ev[ts][:], in0=stg_x[sl][:, ts * 512:(ts + 1) * 512],
                                         in1=pb[4 + ts].ap()[:])
                    ss(v, i, v_self)
                    for ts in range(2):
                        i = v.tensor_mul(out=dstbuf.ap()[:, a * 1024 + ts * 512: a * 1024 + (ts + 1) * 512],
                                         in0=stg_ev[ts][:], in1=pb[6 + ts].ap()[:])
                    inc(i, c_ln)
        nc.all_engine_barrier()

    def emit_aux(tag, reload_a=False):
        b_pe, b_ve, b_se = C(c_pe), C(c_ve), C(c_se)
        with nc.Block(f"aux_{tag}") as blk:
            if reload_a:
                @blk.gpsimd
                def _(g):
                    for h in range(2):
                        i = g.dma_start(out=a1_ball_src.ap().rearrange("p (a m) -> p a m", a=HT)[:, 16 * h:16 * (h + 1), :],
                                        in_=aallT.rearrange("(a p) m -> p a m", p=128)[:, 16 * h:16 * (h + 1), :])
                        inc(i, s_cst[1], 16)

            @blk.tensor
            def _(t):
                if reload_a:
                    t.wait_ge(s_cst[1], C(s_cst[1]))
                for ts in range(2):
                    for k in range(HT):
                        t.matmul(pb[ts][:], a1_ball_src[:, k * 128:(k + 1) * 128],
                                 xln[:, k * 1024 + ts * 512: k * 1024 + ts * 512 + 512],
                                 start=(k == 0), stop=(k == HT - 1))
                for ts in range(2):
                    for k in range(HT):
                        i = t.matmul(pb[2 + ts].ap()[0:16, :], st_rw[:, k * 16:(k + 1) * 16],
                                     xln[:, k * 1024 + ts * 512: k * 1024 + ts * 512 + 512],
                                     start=(k == 0), stop=(k == HT - 1))
                inc(i, c_pe)
                t.wait_ge(c_ve, b_ve + 1)
                for ts in range(2):
                    i = t.matmul(pb[4 + ts].ap()[0:1, :], st_ones16[:],
                                 st_pmask[:, ts * 512:(ts + 1) * 512], start=True, stop=True)
                inc(i, c_pe)
                t.wait_ge(c_ve, b_ve + 2)
                for ts in range(2):
                    i = t.matmul(pb[6 + ts].ap()[0:16, :],
                                 st_onesrow.ap()[0:1, 0:16], st_rowt.ap()[0:1, ts * 512:(ts + 1) * 512],
                                 start=True, stop=True)
                inc(i, c_pe)
                t.wait_ge(c_ve, b_ve + 3)
                for ts in range(2):
                    i = t.matmul(pb[ts][:], st_rexp[:], st_alpha[:, ts * 512:(ts + 1) * 512],
                                 start=True, stop=True)
                inc(i, c_pe)

            @blk.scalar
            def _(s):
                s.wait_ge(c_pe, b_pe + 1)
                for ts in range(2):
                    i = s.activation(out=st_pmask[:, ts * 512:(ts + 1) * 512], in_=pb[2 + ts].ap()[0:16, :],
                                     func=AF.Exp, bias=st_bias16[:])
                inc(i, c_se)
                for ts in range(2):
                    i = s.copy(out=stg_ev[2 + ts][:], in_=pb[ts].ap()[:])
                inc(i, c_se)

            @blk.vector
            def _(v):
                v.wait_ge(c_se, b_se + 1)
                i = v.tensor_scalar(out=st_pmask[:], in0=st_pmask[:], scalar1=st_mcol[:], scalar2=None, op0=OP.mult)
                inc(i, c_ve)
                v.wait_ge(c_pe, b_pe + 2)
                for ts in range(2):
                    i = v.reciprocal(out=st_rowt.ap()[0:1, ts * 512:(ts + 1) * 512], in_=pb[4 + ts].ap()[0:1, :])
                inc(i, c_ve)
                v.wait_ge(c_pe, b_pe + 3)
                v.wait_ge(c_se, b_se + 2)
                for ts in range(2):
                    i = v.tensor_mul(out=st_alpha[:, ts * 512:(ts + 1) * 512],
                                     in0=st_pmask[:, ts * 512:(ts + 1) * 512], in1=pb[6 + ts].ap()[0:16, :])
                inc(i, c_ve)
                v.wait_ge(c_pe, b_pe + 4)
                for ts in range(2):
                    i = v.tensor_mul(out=st_zs[:, ts * 512:(ts + 1) * 512], in0=stg_ev[2 + ts][:],
                                     in1=pb[ts].ap()[:])
                inc(i, c_ve)
        nc.all_engine_barrier()

    def stg_r(q):
        if q < 4:
            return stg_x[q].ap().bitcast(F32)[:, 0:512]
        if q == 4:
            return stg_sq[0].ap().bitcast(F32)[:, 0:512]
        if q < 7:
            return gstage[q - 5].ap().bitcast(F32)[:, 0:512]
        return xrow[0].ap().bitcast(F32)[:, 0:512]

    op_ps = at("op_ps", [128, 8 * 512], F32, A1)
    op_x1 = at("op_x1", [128, 8 * 512], F32, A1 + 16384)

    def ops_ps(q):
        return op_ps.ap()[:, q * 512:(q + 1) * 512]

    def ops_x1(q):
        return op_x1.ap()[:, q * 512:(q + 1) * 512]

    def emit_ffn(tag, wdram, res3, dst3, pre=None):
        NS = 16
        wre = wdram.rearrange("(a p) m -> p a m", p=128)
        b_sd, b_se, b_ve = C(c_sd), C(c_se), C(c_ve)
        b_wl = pre if pre is not None else [C(s) for s in s_wl]
        b_st = [C(s) for s in s_st]
        b_xl = [C(s) for s in s_xl]
        b_sp = [C(s) for s in s_sp]

        def wload(g, u_expr, par):
            for p in range(4):
                g.dma_start(out=ws[par].ap().rearrange("p (a m) -> p a m", a=32)[:, 8 * p:8 * (p + 1), :],
                            in_=wre[:, 8 * p:8 * (p + 1), bass.ds(u_expr * 256, 256)]).then_inc(s_wl[par], 16)
            g.dma_start(out=stg_ball[par][:], in_=ballT[:, bass.ds(u_expr * 256, 256)]).then_inc(s_wl[par], 16)

        def staging(sy, u_expr, par):
            sy.dma_start(out=stg_bb[par][:], in_=bbh_d.ap()[:, bass.ds(u_expr * 2, 2)]).then_inc(s_xl[par], 16)
            for j in range(2):
                for ts in range(2):
                    sy.dma_start(out=stg_r(par * 4 + j * 2 + ts),
                                 in_=res3[bass.ds(u_expr * 2 + j, 1), :, ts * 512:(ts + 1) * 512].squeeze(0)).then_inc(s_xl[par], 16)

        def stores(sy, u_expr, par):
            for j in range(2):
                for ts in range(2):
                    sy.dma_start(out=dst3[bass.ds(u_expr * 2 + j, 1), :, ts * 512:(ts + 1) * 512].squeeze(0),
                                 in_=stg_r(par * 4 + j * 2 + ts)).then_inc(s_sp[par], 16)

        with nc.Block(f"ffn_{tag}") as blk:
            @blk.gpsimd
            def _(g):
                def wl_static(u, par):
                    for p in range(4):
                        g.dma_start(out=ws[par].ap().rearrange("p (a m) -> p a m", a=32)[:, 8 * p:8 * (p + 1), :],
                                    in_=wre[:, 8 * p:8 * (p + 1), u * 256:u * 256 + 256]).then_inc(s_wl[par], 16)
                    g.dma_start(out=stg_ball[par][:], in_=ballT[:, u * 256:u * 256 + 256]).then_inc(s_wl[par], 16)
                    cnt[id(s_wl[par])] += 80
                if pre is None:
                    for u in range(2):
                        wl_static(u, u)
                for u in range(2, NS):
                    par = u % 2
                    g.wait_ge(c_sd, b_sd + u - 1)
                    wl_static(u, par)

            @blk.sync
            def _(sy):
                def staging(u, par):
                    sy.dma_start(out=stg_bb[par][:], in_=bbh_d.ap()[:, u * 2:u * 2 + 2]).then_inc(s_xl[par], 16)
                    for j in range(2):
                        for ts in range(2):
                            sy.dma_start(out=stg_r(par * 4 + j * 2 + ts),
                                         in_=res3[u * 2 + j][:, ts * 512:(ts + 1) * 512]).then_inc(s_xl[par], 16)
                    cnt[id(s_xl[par])] += 80

                def stores(u, par):
                    for j in range(2):
                        for ts in range(2):
                            sy.dma_start(out=dst3[u * 2 + j][:, ts * 512:(ts + 1) * 512],
                                         in_=stg_r(par * 4 + j * 2 + ts)).then_inc(s_sp[par], 16)
                    cnt[id(s_sp[par])] += 64

                for u in range(2):
                    staging(u, u)
                for u in range(NS):
                    par = u % 2
                    sy.wait_ge(c_ve, b_ve + 4 * u + 4)
                    stores(u, par)
                    if u + 2 < NS:
                        staging(u + 2, par)
                sy.wait_ge(s_sp[0], b_sp[0] + 64 * (NS // 2))
                sy.wait_ge(s_sp[1], b_sp[1] + 64 * (NS // 2))

            @blk.tensor
            def _(t):
                bump(t, c_se, 4)
                b_se2 = b_se + 4
                with t.register(f"t{tag}") as r:
                    with t.Fori(0, NS // 2) as s2:
                        for par in range(2):
                            aw(t, r, (s2, 80), b_wl[par] + 80, s_wl[par])
                            aw(t, r, (s2, 8), b_se2 + 4 * par - 4, c_se)
                            for j in range(2):
                                for k in range(32):
                                    for ts in range(2):
                                        t.matmul(pb[par * 4 + j * 2 + ts][:],
                                                 ws[par][:, k * 256 + j * 128: k * 256 + j * 128 + 128],
                                                 xln[:, k * 1024 + ts * 512: k * 1024 + ts * 512 + 512],
                                                 start=(k == 0), stop=False)
                                for ts in range(2):
                                    mm = t.matmul(pb[par * 4 + j * 2 + ts][:],
                                                  stg_ball[par][:, j * 128:(j + 1) * 128],
                                                  st_zs[:, ts * 512:(ts + 1) * 512],
                                                  start=False, stop=True)
                            mm.then_inc(c_sd, 1)
                cnt[id(c_sd)] += NS

            @blk.scalar
            def _(s):
                with s.register(f"s{tag}") as r:
                    with s.Fori(0, NS // 2) as s2:
                        for par in range(2):
                            aw(s, r, (s2, 2), b_sd + par + 1, c_sd)
                            aw(s, r, (s2, 8), b_ve + 4 * par, c_ve)
                            aw(s, r, (s2, 80), b_xl[par] + 16, s_xl[par])
                            for j in range(2):
                                for ts in range(2):
                                    i = s.activation(out=stg_ev[j * 2 + ts][:], in_=pb[par * 4 + j * 2 + ts].ap()[:],
                                                     func=AF.Identity, bias=stg_bb[par][:, j:j + 1], scale=0.5)
                                    i.then_inc(c_se, 1)
                cnt[id(c_se)] += NS * 4

            @blk.vector
            def _(v):
                with v.register(f"v{tag}") as r:
                    with v.Fori(0, NS // 2) as s2:
                        for par in range(2):
                            for j in range(2):
                                for ts in range(2):
                                    q = j * 2 + ts
                                    aw(v, r, (s2, 8), b_se + 4 + 4 * par + q + 1, c_se)
                                    aw(v, r, (s2, 80), b_xl[par] + 16 + 16 * q + 16, s_xl[par])
                                    i = v.tensor_add(out=stg_r(par * 4 + q), in0=stg_ev[q][:], in1=stg_r(par * 4 + q))
                                    i.then_inc(c_ve, 1)
                cnt[id(c_ve)] += NS * 4
        nc.all_engine_barrier()

    def emit_ip_gate(pre=None):
        NS = 32
        b_sd, b_se = C(c_sd), C(c_se)
        b_wl = pre if pre is not None else [C(s) for s in s_wl]

        def wload(g, u_expr, par):
            for p in range(4):
                g.dma_start(out=ws[par].ap().rearrange("p (a m) -> p a m", a=32)[:, 8 * p:8 * (p + 1), :],
                            in_=ipw3[:, 8 * p:8 * (p + 1), bass.ds(u_expr * 256, 256)]).then_inc(s_wl[par], 16)

        with nc.Block("ipg") as blk:
            @blk.gpsimd
            def _(g):
                def wl_static(u, par):
                    for p in range(4):
                        g.dma_start(out=ws[par].ap().rearrange("p (a m) -> p a m", a=32)[:, 8 * p:8 * (p + 1), :],
                                    in_=ipw3[:, 8 * p:8 * (p + 1), u * 256:u * 256 + 256]).then_inc(s_wl[par], 16)
                    cnt[id(s_wl[par])] += 64
                if pre is None:
                    for u in range(2):
                        wl_static(u, u)
                for u in range(2, NS):
                    g.wait_ge(c_sd, b_sd + u - 1)
                    wl_static(u, u % 2)

            @blk.tensor
            def _(t):
                bump(t, c_se, 4)
                with t.register("tipg") as r:
                    with t.Fori(0, NS // 2) as s2:
                        for par in range(2):
                            aw(t, r, (s2, 64), b_wl[par] + 64, s_wl[par])
                            aw(t, r, (s2, 4), b_se + 2 + 2 * par, c_se)
                            for j in range(2):
                                for k in range(32):
                                    for ts in range(2):
                                        mm = t.matmul(pb[par * 4 + j * 2 + ts][:],
                                                      ws[par][:, k * 256 + j * 128: k * 256 + j * 128 + 128],
                                                      xln[:, k * 1024 + ts * 512: k * 1024 + ts * 512 + 512],
                                                      start=(k == 0), stop=(k == 31))
                            mm.then_inc(c_sd, 1)
                cnt[id(c_sd)] += NS

            @blk.scalar
            def _(s):
                b_sp2 = [C(x) for x in s_sp]
                with s.register("sipg") as r:
                    with s.Fori(0, NS // 2) as s2:
                        for par in range(2):
                            aw(s, r, (s2, 2), b_sd + par + 1, c_sd)
                            if True:
                                aw(s, r, (s2, 32), b_sp2[1 - par] + 32 * par, s_sp[1 - par])
                            for j in range(2):
                                for ts in range(2):
                                    i = s.activation(out=gstage[j][:, ts * 512:(ts + 1) * 512],
                                                     in_=pb[par * 4 + j * 2 + ts].ap()[:], func=AF.Silu)
                                    if ts == 1:
                                        i.then_inc(c_se, 1)
                cnt[id(c_se)] += NS * 2

            @blk.sync
            def _(sy):
                for u in range(NS):
                    par = u % 2
                    for j in range(2):
                        sy.wait_ge(c_se, b_se + 4 + 2 * u + j + 1)
                        sy.dma_start(out=gsp[u * 2 + j], in_=gstage[j][:]).then_inc(s_sp[par], 16)
                        cnt[id(s_sp[par])] += 16
                sy.wait_ge(s_sp[0], C(s_sp[0]))
                sy.wait_ge(s_sp[1], C(s_sp[1]))
        nc.all_engine_barrier()

    def emit_ip_xbc():
        NV = 40
        b_sd, b_se, b_pe, b_cv, b_ve = C(c_sd), C(c_se), C(c_pe), C(c_cv), C(c_ve)
        b_wl = [C(s) for s in s_wl]
        b_st = [C(s) for s in s_st]
        b_sp = [C(s) for s in s_sp]

        def wload(g, v_expr, par):
            for p in range(4):
                g.dma_start(out=ws[par].ap().rearrange("p (a m) -> p a m", a=32)[:, 8 * p:8 * (p + 1), :],
                            in_=ipw3[:, 8 * p:8 * (p + 1), bass.ds(v_expr * 256 + 8192, 256)]).then_inc(s_wl[par], 16)

        with nc.Block("ipx") as blk:
            @blk.gpsimd
            def _(g):
                def wl_static(v, par):
                    for p in range(4):
                        g.dma_start(out=ws[par].ap().rearrange("p (a m) -> p a m", a=32)[:, 8 * p:8 * (p + 1), :],
                                    in_=ipw3[:, 8 * p:8 * (p + 1), v * 256 + 8192:v * 256 + 8448]).then_inc(s_wl[par], 16)
                    cnt[id(s_wl[par])] += 64
                for v in range(2):
                    wl_static(v, v)
                for v in range(2, NV):
                    g.wait_ge(c_sd, b_sd + v - 1)
                    wl_static(v, v % 2)

            @blk.vector
            def _(v_):
                v_.memset(xrow[0].ap()[:, 0:3], 0.0)
                i = v_.memset(xrow[1].ap()[:, 0:3], 0.0)
                inc(i, c_ve)
                with v_.register("vipx") as r:
                    with v_.Fori(0, NV) as v1:
                        aw(v_, r, (v1, 16), b_st[0] + 16, s_st[0])
                        aw(v_, r, (v1, 1), b_pe, c_pe)
                        for j in range(2):
                            i = v_.tensor_tensor(
                                out=diag4.ap().rearrange("p (j k m) -> p j k m", j=2, k=4)[:, j],
                                in0=st_identbf.ap().unsqueeze(1).broadcast_to([128, 4, 128]),
                                in1=stg_cw[0][:, j * 4:(j + 1) * 4].unsqueeze(2).broadcast_to([128, 4, 128]),
                                op=OP.mult)
                        i.then_inc(c_ve, 1)
                cnt[id(c_ve)] += NV

            @blk.sync
            def _(sy):
                sy.dma_start(out=stg_cw[0][:], in_=cwT[:, 0:8]).then_inc(s_st[0], 16)
                sy.dma_start(out=stg_cb[0][:], in_=cbT[:, 0:2]).then_inc(s_st[1], 16)
                cnt[id(s_st[0])] += 16
                cnt[id(s_st[1])] += 16
                for v1 in range(NV):
                    if v1 + 1 < NV:
                        sy.wait_ge(c_ve, b_ve + 1 + v1 + 1)
                        sy.dma_start(out=stg_cw[0][:], in_=cwT[:, (v1 + 1) * 8:(v1 + 1) * 8 + 8]).then_inc(s_st[0], 16)
                        cnt[id(s_st[0])] += 16
                    sy.wait_ge(c_cv, b_cv + 2 + v1 + 1)
                    if v1 + 1 < NV:
                        sy.dma_start(out=stg_cb[0][:], in_=cbT[:, (v1 + 1) * 2:(v1 + 1) * 2 + 2]).then_inc(s_st[1], 16)
                        cnt[id(s_st[1])] += 16
                    for j in range(2):
                        sy.dma_start(out=xcsp[v1 * 2 + j], in_=gstage[j][:]).then_inc(s_sp[0], 16)
                    cnt[id(s_sp[0])] += 32
                    sy.wait_ge(c_se, b_se + v1 + 1)
                    sy.dma_start(out=ctxin[:, v1 * 6:v1 * 6 + 6], in_=stg_smb[:, (v1 % 2) * 6:(v1 % 2) * 6 + 6]).then_inc(s_st[1], 16)
                    sy.dma_start(out=raw03_d[:, v1 * 6:v1 * 6 + 6], in_=stg_sm6[:, (v1 % 2) * 6:(v1 % 2) * 6 + 6]).then_inc(s_st[1], 16)
                    cnt[id(s_st[1])] += 32
                sy.wait_ge(s_sp[0], C(s_sp[0]))

            @blk.tensor
            def _(t):
                bump(t, c_cv, 2)
                b_cv2 = b_cv + 2
                mmpar = [0]

                def main_mms():
                    mm = None
                    for j in range(2):
                        for k in range(32):
                            for ts in range(2):
                                mm = t.matmul(pb[j * 2 + ts][:],
                                              ws[mmpar[0]][:, k * 256 + j * 128: k * 256 + j * 128 + 128],
                                              xln[:, k * 1024 + ts * 512: k * 1024 + ts * 512 + 512],
                                              start=(k == 0), stop=(k == 31))
                    mm.then_inc(c_sd, 1)

                def conv_mms():
                    mm = None
                    for j in range(2):
                        for ts in range(2):
                            for k in range(4):
                                mm = t.matmul(pb[4 + j * 2 + ts][:],
                                              diag4[:, j * 512 + k * 128: j * 512 + (k + 1) * 128],
                                              xrow[j].ap()[:, ts * 512 + k: ts * 512 + k + 512],
                                              start=(k == 0), stop=(k == 3))
                    mm.then_inc(c_pe, 1)

                with t.register("tipx") as r:
                    t.wait_ge(s_wl[0], b_wl[0] + 64)
                    main_mms()
                    with t.Fori(0, NV // 2 - 1) as s2:
                        mmpar[0] = 1
                        aw(t, r, (s2, 64), b_wl[1] + 64, s_wl[1])
                        aw(t, r, (s2, 2), b_se + 1, c_se)
                        main_mms()
                        aw(t, r, (s2, 2), b_ve + 1 + 1, c_ve)
                        aw(t, r, (s2, 2), b_cv2, c_cv)
                        conv_mms()
                        mmpar[0] = 0
                        aw(t, r, (s2, 64), b_wl[0] + 128, s_wl[0])
                        aw(t, r, (s2, 2), b_se + 2, c_se)
                        main_mms()
                        aw(t, r, (s2, 2), b_ve + 2 + 1, c_ve)
                        aw(t, r, (s2, 2), b_cv2 + 1, c_cv)
                        conv_mms()
                    mmpar[0] = 1
                    t.wait_ge(s_wl[1], b_wl[1] + 64 * (NV // 2))
                    t.wait_ge(c_se, b_se + NV - 1)
                    main_mms()
                    t.wait_ge(c_ve, b_ve + (NV - 1) + 1)
                    t.wait_ge(c_cv, b_cv2 + NV - 2)
                    conv_mms()
                    t.wait_ge(c_se, b_se + NV)
                    t.wait_ge(c_ve, b_ve + NV + 1)
                    t.wait_ge(c_cv, b_cv2 + NV - 1)
                    conv_mms()
                cnt[id(c_sd)] += NV
                cnt[id(c_pe)] += NV

            @blk.scalar
            def _(s):
                for v1 in range(NV):
                    s.wait_ge(c_pe, b_pe + v1)
                    s.wait_ge(c_sd, b_sd + v1 + 1)
                    if v1 >= 2:
                        s.wait_ge(s_st[1], b_st[1] + 16 + 48 * (v1 - 1))
                    for j in range(2):
                        for ts in range(2):
                            i = s.activation(out=xrow[j].ap()[:, 3 + ts * 512: 3 + (ts + 1) * 512],
                                             in_=pb[j * 2 + ts].ap()[:], func=AF.Copy)
                    ss(s, i, s_self)
                    sm_off = (v1 % 2) * 6
                    for j in range(2):
                        s.copy(out=stg_smb[:, sm_off + j * 3:sm_off + (j + 1) * 3], in_=xrow[j].ap()[:, 3 + 1021: 3 + 1024])
                        i = s.copy(out=stg_sm6[:, sm_off + j * 3:sm_off + (j + 1) * 3], in_=xrow[j].ap()[:, 3: 6])
                    i.then_inc(c_se, 1)
                    cnt[id(c_se)] += 1
                    s.wait_ge(c_pe, b_pe + v1 + 1)
                    if v1 >= 1:
                        s.wait_ge(s_sp[0], b_sp[0] + 32 * v1)
                    for j in range(2):
                        for ts in range(2):
                            i = s.activation(out=gstage[j][:, ts * 512:(ts + 1) * 512],
                                             in_=pb[4 + j * 2 + ts].ap()[:], func=AF.Silu,
                                             bias=stg_cb[0][:, j:j + 1])
                    i.then_inc(c_cv, 1)
                    cnt[id(c_cv)] += 1
        nc.all_engine_barrier()

    def emit_dt():
        with nc.Block("dt") as blk:
            @blk.gpsimd
            def _(g):
                for p in range(4):
                    i = g.dma_start(out=ws[0].ap().rearrange("p (a m) -> p a m", a=32)[:, 8 * p:8 * (p + 1), 0:128],
                                    in_=ipw3[:, 8 * p:8 * (p + 1), 18432:18560])
                    inc(i, s_wl[0], 16)

            @blk.tensor
            def _(t):
                t.wait_ge(s_wl[0], C(s_wl[0]))
                for ts in range(2):
                    for k in range(32):
                        i = t.matmul(pb[ts].ap()[:], ws[0][:, k * 256:k * 256 + 128],
                                     xln[:, k * 1024 + ts * 512: k * 1024 + ts * 512 + 512],
                                     start=(k == 0), stop=(k == 31))
                inc(i, c_pe)

            @blk.vector
            def _(v):
                v.wait_ge(c_pe, C(c_pe))
                v.tensor_copy(out=st_dtraw.ap()[:, 0:512], in_=pb[0].ap()[:])
                i = v.tensor_copy(out=st_dtraw.ap()[:, 512:1024], in_=pb[1].ap()[:])
                inc(i, c_ve)
        nc.all_engine_barrier()

        tmp = [sca_p[0].ap()[:, i * 1024:(i + 1) * 1024] for i in range(6)]
        tz = sca_p[1].ap()[:, 0:128]
        with nc.Block("dtp") as blk:
            @blk.scalar
            def _(s):
                i = s.activation(out=tmp[0], in_=st_dtraw[:], func=AF.Identity, bias=st_dtb[:])
                ss(s, i, s_self)
                i = s.activation(out=tmp[1], in_=tmp[0], func=AF.Abs)
                ss(s, i, s_self)
                i = s.activation(out=tmp[2], in_=tmp[1], func=AF.Exp, scale=-1.0)
                ss(s, i, s_self)
                i = s.activation(out=tmp[3], in_=tmp[2], func=AF.Ln, bias=st_konst.ap()[:, 2:3])
                inc(i, c_se)

            @blk.vector
            def _(v):
                v.wait_ge(c_se, C(c_se))
                i = v.tensor_scalar_max(out=tmp[4], in0=tmp[0], scalar1=0.0)
                ss(v, i, v_self)
                i = v.tensor_add(out=st_dtv[:], in0=tmp[4], in1=tmp[3])
                ss(v, i, v_self)
                i = v.tensor_scalar(out=tmp[5], in0=st_dtv[:], scalar1=st_acol[:], scalar2=-1.0,
                                    op0=OP.mult, op1=OP.mult)
                ss(v, i, v_self)
                i = v.memset(tz, 0.0)
                ss(v, i, v_self)
                for c in range(NCH):
                    i = v.tensor_tensor_scan(out=st_cum.ap()[:, c * 128:(c + 1) * 128],
                                             data0=sca_p[0].ap()[:, 5 * 1024 + c * 128: 5 * 1024 + (c + 1) * 128],
                                             data1=tz, initial=0.0, op0=OP.add, op1=OP.add)
                inc(i, c_ve)

            @blk.tensor
            def _(t):
                t.wait_ge(c_ve, C(c_ve))
                b_ve_t = C(c_ve)
                for c in range(NCH):
                    if c >= 2:
                        t.wait_ge(c_ve, b_ve_t + c - 1)
                    t.transpose(pb[c % 2].ap()[:, 0:128], st_cum.ap()[:, c * 128:(c + 1) * 128], st_identf[:])
                    i = t.transpose(pb[2 + c % 2].ap()[:, 0:128], st_dtv.ap()[:, c * 128:(c + 1) * 128], st_identf[:])
                    inc(i, c_pe)

            @blk.vector
            def _(v2):
                b_pe = C(c_pe) - NCH
                for c in range(NCH):
                    v2.wait_ge(c_pe, b_pe + c + 1)
                    v2.tensor_copy(out=st_cumT.ap()[:, c * 128:(c + 1) * 128], in_=pb[c % 2].ap()[:, 0:128])
                    i = v2.tensor_copy(out=st_dtvT.ap()[:, c * 128:(c + 1) * 128], in_=pb[2 + c % 2].ap()[:, 0:128])
                    inc(i, c_ve)
        nc.all_engine_barrier()

        with nc.Block("dtp2a") as blk:
            @blk.scalar
            def _(s):
                i = s.activation(out=st_sdT[:], in_=st_cumT[:], func=AF.Exp)
                inc(i, c_se)
        b2 = {"pe": C(c_pe), "se": C(c_se), "ve": C(c_ve)}

        def s_copy_pos(c):
            return 1 if c == 0 else (3 if c == 1 else 3 * c - 1)

        def s_exp_pos(c):
            return s_copy_pos(c) + 1

        def s_dtok_pos(j):
            return 3 * j + 7 if j <= 5 else 23 + (j - 6)

        with nc.Block("dtp2b") as blk:
            @blk.tensor
            def _(t):
                t.wait_ge(c_se, b2["se"])
                for c in range(NCH):
                    if c >= 2:
                        t.wait_ge(c_se, b2["se"] + s_exp_pos(c - 2))
                    i = t.transpose(pb[4 + c % 2].ap()[0:1, 0:128],
                                    st_cum.ap()[:, c * 128 + 127: c * 128 + 128], st_identf[:])
                    inc(i, c_pe)
                    if c >= 1:
                        t.wait_ge(c_se, b2["se"] + s_copy_pos(c - 1))
                        if c >= 3:
                            t.wait_ge(c_ve, b2["ve"] + c - 2)
                        i = t.matmul(pb[6 + (c - 1) % 2][:, 0:128], st_onesrow[:],
                                     st_cend.ap()[0:1, ((c - 1) % 2) * 128: ((c - 1) % 2) * 128 + 128],
                                     start=True, stop=True)
                        inc(i, c_pe)
                t.wait_ge(c_se, b2["se"] + s_copy_pos(NCH - 1))
                t.wait_ge(c_ve, b2["ve"] + NCH - 2)
                i = t.matmul(pb[6 + (NCH - 1) % 2][:, 0:128], st_onesrow[:],
                             st_cend.ap()[0:1, ((NCH - 1) % 2) * 128: ((NCH - 1) % 2) * 128 + 128],
                             start=True, stop=True)
                inc(i, c_pe)

            @blk.scalar
            def _(s2):
                def dtok(j):
                    s2.wait_ge(c_ve, b2["ve"] + j + 1)
                    i = s2.activation(out=st_dtok.ap()[:, j * 128:(j + 1) * 128],
                                      in_=sca_p[1].ap()[:, 1024 + (j % 2) * 128: 1024 + (j % 2) * 128 + 128],
                                      func=AF.Exp)
                    inc(i, c_se)
                for c in range(NCH):
                    s2.wait_ge(c_pe, b2["pe"] + 2 * c + (1 if c == 0 else 0))
                    i = s2.copy(out=st_cend.ap()[0:1, (c % 2) * 128:(c % 2) * 128 + 128],
                                in_=pb[4 + c % 2].ap()[0:1, 0:128])
                    inc(i, c_se)
                    i = s2.activation(out=st_grow.ap()[0:1, c * 128:(c + 1) * 128],
                                      in_=pb[4 + c % 2].ap()[0:1, 0:128], func=AF.Exp)
                    inc(i, c_se)
                    if c >= 2:
                        dtok(c - 2)
                dtok(NCH - 2)
                dtok(NCH - 1)
                assert C(c_se) == b2["se"] + 24

            @blk.vector
            def _(v):
                for c in range(NCH):
                    v.wait_ge(c_pe, b2["pe"] + (2 * c + 3 if c < NCH - 1 else 2 * NCH))
                    if c >= 2:
                        v.wait_ge(c_se, b2["se"] + s_dtok_pos(c - 2))
                    i = v.tensor_sub(out=sca_p[1].ap()[:, 1024 + (c % 2) * 128: 1024 + (c % 2) * 128 + 128],
                                     in0=pb[6 + c % 2].ap()[:, 0:128], in1=st_cumT.ap()[:, c * 128:(c + 1) * 128])
                    inc(i, c_ve)
        nc.all_engine_barrier()

        b3 = {"pe": C(c_pe)}
        with nc.Block("dtp3") as blk:
            @blk.vector
            def _(v):
                i = v.memset(st_gpre.ap()[0:1, 0:128], 1.0)
                ss(v, i, v_self)
                for c in range(1, NCH):
                    i = v.tensor_mul(out=st_gpre.ap()[0:1, c * 128:(c + 1) * 128],
                                     in0=st_gpre.ap()[0:1, (c - 1) * 128:c * 128],
                                     in1=st_grow.ap()[0:1, (c - 1) * 128:c * 128])
                    ss(v, i, v_self)
                i = v.tensor_tensor(out=st_dtv2T[:], in0=st_dtvT[:], in1=st_dtok[:], op=OP.mult)
                inc(i, c_ve)

            @blk.tensor
            def _(t):
                for half in range(2):
                    i = t.matmul(pb[2 + half].ap()[:, 0:512], st_onesrow[:],
                                 st_grow.ap()[0:1, half * 512:(half + 1) * 512],
                                 start=True, stop=True)
                    inc(i, c_pe)

            @blk.scalar
            def _(s):
                for half in range(2):
                    s.wait_ge(c_pe, b3["pe"] + half + 1)
                    i = s.copy(out=st_gamT.ap()[:, half * 512:(half + 1) * 512],
                               in_=pb[2 + half].ap()[:, 0:512])
                    inc(i, c_se)
        nc.all_engine_barrier()

    def emit_ctx():
        with nc.Block("ctx") as blk:
            @blk.sync
            def _(sy):
                i = sy.dma_start(out=st_raw03[:], in_=raw03_d[:, :])
                inc(i, s_rb[0], 16)
                sy.wait_ge(s_rb[0], C(s_rb[0]))

            @blk.gpsimd
            def _(g):
                g.wait_ge(s_rb[0], C(s_rb[0]))
                i = g.collective_compute("AllGather", OP.bypass, replica_groups=[[0, 1], [2, 3], [4, 5], [6, 7]],
                                         ins=[ctxin.ap().opt()], outs=[ctxout.ap().opt()])
                inc(i, s_cc, 1)

            @blk.sync
            def _(sy2):
                sy2.wait_ge(s_cc, C(s_cc))
                i = sy2.dma_start(out=st_ctxr[:], in_=ctxout[0:128, :])
                inc(i, s_rb[0], 16)

            @blk.vector
            def _(v):
                v.wait_ge(s_rb[0], C(s_rb[0]))
                i = v.tensor_scalar(out=st_ctxr[:], in0=st_ctxr[:], scalar1=st_cmask[:], scalar2=None, op0=OP.mult)
                inc(i, c_ve)
        nc.all_engine_barrier()

        def vw(tnsr, off, n):
            return tnsr.ap().rearrange("p (a q) -> p a q", a=80)[:, :, off:off + n]

        with nc.Block("patch") as blk:
            @blk.vector
            def _(v):
                cw4 = st_cw.ap().rearrange("p (a k) -> p a k", a=80)
                pview = st_patch.ap().rearrange("p (a q) -> p a q", a=80)
                p2view = st_patch2.ap().rearrange("p (a q) -> p a q", a=80)
                first = True
                for k in range(3):
                    n = 3 - k
                    wv = cw4[:, :, k:k + 1].broadcast_to([128, 80, n])
                    if first:
                        i = v.tensor_tensor(out=pview[:, :, 0:n], in0=vw(st_ctxr, k, n), in1=wv, op=OP.mult)
                        ss(v, i, v_self)
                        first = False
                    else:
                        i = v.tensor_tensor(out=p2view[:, :, 0:n], in0=vw(st_ctxr, k, n), in1=wv, op=OP.mult)
                        ss(v, i, v_self)
                        i = v.tensor_add(out=pview[:, :, 0:n], in0=pview[:, :, 0:n], in1=p2view[:, :, 0:n])
                        ss(v, i, v_self)
                for k in range(1, 4):
                    lo = max(0, 3 - k)
                    n = 3 - lo
                    if n <= 0:
                        continue
                    r0 = lo + k - 3
                    wv = cw4[:, :, k:k + 1].broadcast_to([128, 80, n])
                    i = v.tensor_tensor(out=p2view[:, :, lo:lo + n], in0=vw(st_raw03, r0, n), in1=wv, op=OP.mult)
                    ss(v, i, v_self)
                    i = v.tensor_add(out=pview[:, :, lo:lo + n], in0=pview[:, :, lo:lo + n], in1=p2view[:, :, lo:lo + n])
                    ss(v, i, v_self)
                cb3 = st_cb.ap().unsqueeze(2).broadcast_to([128, 80, 3])
                i = v.tensor_add(out=pview, in0=pview, in1=cb3)
                inc(i, c_ve)

            @blk.scalar
            def _(s):
                s.wait_ge(c_ve, C(c_ve))
                i = s.activation(out=st_patchb[:], in_=st_patch[:], func=AF.Silu)
                inc(i, c_se)
        nc.all_engine_barrier()

    def emit_sca():
        with nc.Block("scaz") as blk:
            @blk.vector
            def _(v):
                i = v.memset(sca_p[0][:], 0.0)
                inc(i, c_ve)
        nc.all_engine_barrier()

        B_pe = C(c_pe); B_ve = C(c_ve); B_se = C(c_se); B_cv = C(c_cv)
        B_rb = [C(s_rb[0]), C(s_rb[1])]; B_rc = [C(s_rc[0]), C(s_rc[1])]
        B_rb2 = C(s_rb[2]); B_cst = C(s_cst[0])

        def pe0(c):
            return B_pe + 88 * c

        def ev0(c):
            return B_ve + 48 * c + 1

        psH_g = psH.ap().bitcast(BF16).rearrange("p (k q m) -> p k q m", k=4, q=16, m=64)[:, :, 0:2, :]
        psL_bt = psL.ap().bitcast(BF16).rearrange("p (k q m) -> p k q m", k=4, q=8, m=128)[:, 2:4, 0, :]

        with nc.Block("sca") as blk:
            @blk.sync
            def _(sy):
                def rb(c):
                    w = c % 2
                    if c >= 2:
                        sy.wait_ge(c_pe, pe0(c - 2) + 64)
                    for p in range(8):
                        i = sy.dma_start(out=sca_xcw[w].ap().rearrange("p (a m) -> p a m", a=64)[:, 8 * p:8 * (p + 1), :],
                                         in_=xcsp.rearrange("a p m -> p a m")[:, 8 * p:8 * (p + 1), c * 128:(c + 1) * 128])
                        inc(i, s_rb[w], 16)
                    if c >= 2:
                        sy.wait_ge(c_pe, pe0(c - 2) + 72)
                    i = sy.dma_start(out=sca_bw[w].ap().rearrange("p (a m) -> p a m", a=8),
                                     in_=xcsp.rearrange("a p m -> p a m")[:, 64:72, c * 128:(c + 1) * 128])
                    inc(i, s_rc[w], 16)
                rb(0)
                rb(1)
                for c in range(NCH):
                    if c + 2 < NCH:
                        rb(c + 2)
                    sy.wait_ge(c_ve, ev0(c) + 32)
                    i = sy.dma_start(out=xssp[c], in_=sca_xs[:])
                    inc(i, s_rb[2], 16)
                sy.wait_ge(s_rb[2], B_rb2 + 16 * NCH)

            @blk.gpsimd
            def _(g):
                for c in range(NCH):
                    w = c % 2
                    if c >= 1:
                        g.wait_ge(c_ve, ev0(c - 1) + 48)
                        g.wait_ge(s_cst[0], B_cst + 16 * c)
                    i = g.dma_start(out=plsp[c], in_=sca_p[w][:])
                    inc(i, s_cst[0], 16)
                    if not _GPV:
                        i = g.tensor_tensor(
                            out=sca_p[1 - w].ap().rearrange("p (h m) -> p h m", h=128),
                            in0=sca_p[w].ap().rearrange("p (h m) -> p h m", h=128),
                            in1=st_gamT.ap()[:, c * 128:(c + 1) * 128].unsqueeze(2).broadcast_to([128, 128, 64]),
                            op=OP.mult)
                        inc(i, c_cv)
                g.wait_ge(s_cst[0], B_cst + 16 * NCH)

            @blk.tensor
            def _(t):
                for c in range(NCH):
                    w = c % 2
                    t.wait_ge(s_rb[w], B_rb[w] + 128 * (c // 2 + 1))
                    if c == 0:
                        t.wait_ge(c_ve, B_ve + 1)
                    else:
                        t.wait_ge(c_ve, ev0(c - 1) + 32)
                    for a in range(NXT):
                        if a >= 4 and a % 4 == 0:
                            t.wait_ge(c_ve, ev0(c) + 2 * ((a - 4) // 4 + 1))
                        i = t.transpose(pb[4 + a % 4].ap().bitcast(BF16)[:, 0:128],
                                        sca_xcw[w].ap()[:, a * 128:(a + 1) * 128], st_identbf[:])
                        inc(i, c_pe)
                    assert C(c_pe) == pe0(c) + 64
                    t.wait_ge(s_rc[w], B_rc[w] + 16 * (c // 2 + 1))
                    for gi in range(8):
                        if gi >= 2:
                            t.wait_ge(c_se, B_se + 4 * c + (gi - 2) // 2 + 1)
                        elif c >= 1:
                            t.wait_ge(c_se, B_se + 4 * c)
                        i = t.transpose(pb[2 + gi % 2].ap().bitcast(BF16)[:, 0:128],
                                        sca_bw[w].ap()[:, gi * 128:(gi + 1) * 128], st_identbf[:])
                        inc(i, c_pe)
                    assert C(c_pe) == pe0(c) + 72
                    for hb in range(16):
                        gi = hb // 2
                        t.wait_ge(c_se, B_se + 4 * c + gi // 2 + 1)
                        t.wait_ge(c_ve, ev0(c) + 2 * hb + 2)
                        if hb >= 2:
                            t.wait_ge(c_ve, ev0(c) + 32 + hb - 1)
                        for hq in range(8):
                            h = hb * 8 + hq
                            i = t.matmul(pb[hb % 2].ap()[:, hq * 64:(hq + 1) * 64],
                                         sca_btok.ap()[:, gi * 128:(gi + 1) * 128],
                                         sca_xs2.ap()[:, h * 64:(h + 1) * 64],
                                         start=True, stop=True)
                        inc(i, c_pe)
                    assert C(c_pe) == pe0(c) + 88

            @blk.vector
            def _(v):
                for c in range(NCH):
                    w = c % 2
                    if c == 0:
                        v.wait_ge(s_rb[0], B_rb[0] + 128)
                        v.wait_ge(s_rc[0], B_rc[0] + 16)
                        v.tensor_copy(out=sca_xcw[0].ap().rearrange("p (a m) -> p a m", a=64)[:, :, 0:3],
                                      in_=st_patchb.ap().rearrange("p (a q) -> p a q", a=80)[:, 0:64, :])
                        i = v.tensor_copy(out=sca_bw[0].ap().rearrange("p (a m) -> p a m", a=8)[:, :, 0:3],
                                          in_=st_patchb.ap().rearrange("p (a q) -> p a q", a=80)[:, 64:72, :])
                        inc(i, c_ve)
                    assert C(c_ve) == ev0(c)
                    if _GPV:
                        if c >= 1:
                            v.wait_ge(s_cst[0], B_cst + 16 * c)
                        i = v.tensor_tensor(
                            out=sca_p[1 - w].ap().rearrange("p (h m) -> p h m", h=128),
                            in0=sca_p[w].ap().rearrange("p (h m) -> p h m", h=128),
                            in1=st_gamT.ap()[:, c * 128:(c + 1) * 128].unsqueeze(2).broadcast_to([128, 128, 64]),
                            op=OP.mult)
                        i.then_inc(c_cv, 1)
                        cnt[id(c_cv)] += 1
                    for g in range(16):
                        if g == 0:
                            v.wait_ge(s_rb[2], B_rb2 + 16 * c)
                            if c >= 1:
                                v.wait_ge(c_pe, pe0(c - 1) + 88)
                        v.wait_ge(c_pe, pe0(c) + 4 * (g + 1))
                        i = v.tensor_tensor(
                            out=sca_xs.ap().rearrange("p (g k h m) -> p g k h m", g=16, k=4, h=2)[:, g],
                            in0=psH_g,
                            in1=st_dtvT.ap()[:, c * 128 + 8 * g: c * 128 + 8 * g + 8]
                                .rearrange("p (k h) -> p k h", k=4).unsqueeze(3).broadcast_to([128, 4, 2, 64]),
                            op=OP.mult)
                        inc(i, c_ve)
                        i = v.tensor_tensor(
                            out=sca_xs2.ap().rearrange("p (g k h m) -> p g k h m", g=16, k=4, h=2)[:, g],
                            in0=psH_g,
                            in1=st_dtv2T.ap()[:, c * 128 + 8 * g: c * 128 + 8 * g + 8]
                                .rearrange("p (k h) -> p k h", k=4).unsqueeze(3).broadcast_to([128, 4, 2, 64]),
                            op=OP.mult)
                        inc(i, c_ve)
                    assert C(c_ve) == ev0(c) + 32
                    for hb in range(16):
                        v.wait_ge(c_pe, pe0(c) + 72 + hb + 1)
                        if hb == 0:
                            v.wait_ge(c_cv, B_cv + c + 1)
                        i = v.tensor_add(out=sca_p[1 - w].ap()[:, hb * 512:(hb + 1) * 512],
                                         in0=sca_p[1 - w].ap()[:, hb * 512:(hb + 1) * 512],
                                         in1=pb[hb % 2].ap()[:])
                        inc(i, c_ve)
                    assert C(c_ve) == ev0(c) + 48

            @blk.scalar
            def _(s):
                for c in range(NCH):
                    for g in range(4):
                        s.wait_ge(c_pe, pe0(c) + 64 + 2 * (g + 1))
                        if c >= 1:
                            s.wait_ge(c_pe, pe0(c - 1) + 72 + 4 * g + 4)
                        i = s.copy(out=sca_btok.ap().rearrange("p (g z m) -> p g z m", g=4, z=2)[:, g],
                                   in_=psL_bt)
                        inc(i, c_se)
                    assert C(c_se) == B_se + 4 * (c + 1)
        nc.all_engine_barrier()

    def emit_pex():
        pfin = sca_p[0]
        with nc.Block("pex") as blk:
            @blk.sync
            def _(sy):
                i = sy.dma_start(out=pstin[:, :], in_=pfin[:])
                inc(i, s_rb[0], 16)
                sy.wait_ge(s_rb[0], C(s_rb[0]))
                scb_rb_xcw(sy, 0)
                scb_rb_gtw(sy, 0)

            @blk.gpsimd
            def _(g):
                scb_rb_small(g, 0)
                g.wait_ge(s_rb[0], C(s_rb[0]))
                i = g.collective_compute("AllGather", OP.bypass, replica_groups=[[0, 1], [2, 3], [4, 5], [6, 7]],
                                         ins=[pstin.ap().opt()], outs=[pstout.ap().opt()])
                inc(i, s_cc, 1)

            @blk.sync
            def _(sy2):
                sy2.wait_ge(s_cc, C(s_cc))
                i = sy2.dma_start(out=sca_p[1][:], in_=pstout[0:128, :])
                inc(i, s_rb[0], 16)

            @blk.vector
            def _(v):
                v.wait_ge(s_rb[0], C(s_rb[0]))
                i = v.tensor_scalar(out=sca_p[1][:], in0=sca_p[1][:], scalar1=st_cmask[:], scalar2=None, op0=OP.mult)
                inc(i, c_ve)

            @blk.scalar
            def _(s):
                s.wait_ge(c_ve, C(c_ve))
                i = s.copy(out=scb_pini[:], in_=sca_p[1][:])
                inc(i, c_se)
        nc.all_engine_barrier()

    def scb_marks():
        pe = 0
        gmt_mark = {}
        cumbc_mark = {}
        ydo_mark = {}
        for g in range(8):
            pe += 1; gmt_mark[g] = pe
            for qq in range(4):
                q = g * 4 + qq
                pe += 1; cumbc_mark[q] = pe
                if q >= 1:
                    pe += 1; ydo_mark[q - 1] = pe
        pe += 1; ydo_mark[31] = pe
        npe_head = pe
        ve = 0
        lpre_mark = {}
        y8_mark = {}
        for q in range(32):
            ve += 1; lpre_mark[q] = ve
            if q % 2 == 1:
                ve += 1; y8_mark[(q - 1) // 2] = ve
        nve_head = ve
        se = 0
        gmtev_mark = {}
        lt_mark = {}
        for g in range(8):
            se += 1; gmtev_mark[g] = se
            for qq in range(4):
                se += 1; lt_mark[g * 4 + qq] = se
        nse_head = se
        return dict(gmt=gmt_mark, cumbc=cumbc_mark, ydo=ydo_mark, npe=npe_head,
                    lpre=lpre_mark, y8=y8_mark, nve=nve_head,
                    gmtev=gmtev_mark, lt=lt_mark, nse=nse_head)

    MK = scb_marks()

    def scb_rb_xcw(eng, c):
        for p in range(8):
            i = eng.dma_start(out=scb_xcw.ap().rearrange("p (a m) -> p a m", a=64)[:, 8 * p:8 * (p + 1), :],
                              in_=xcsp.rearrange("a p m -> p a m")[:, 8 * p:8 * (p + 1), c * 128:(c + 1) * 128])
            inc(i, s_rb[0], 16)

    def scb_rb_gtw(eng, c):
        for p in range(8):
            i = eng.dma_start(out=scb_gtw.ap().rearrange("p (a m) -> p a m", a=64)[:, 8 * p:8 * (p + 1), :],
                              in_=gsp.rearrange("a p m -> p a m")[:, 8 * p:8 * (p + 1), c * 128:(c + 1) * 128])
            inc(i, s_rb[1], 16)

    def scb_rb_small(eng, c):
        i = eng.dma_start(out=scb_bcw.ap().rearrange("p (a m) -> p a m", a=16),
                          in_=xcsp.rearrange("a p m -> p a m")[:, 64:80, c * 128:(c + 1) * 128])
        inc(i, s_rg, 16)
        i = eng.dma_start(out=scb_xs[:], in_=xssp[c])
        inc(i, s_rg, 16)
        i = eng.dma_start(out=scb_plrb[:], in_=plsp[c])
        inc(i, s_rg, 16)

    def emit_scb():
        for c in range(NCH):
            with nc.Block(f"sb{c}") as blk:
                b0 = {"rb0": C(s_rb[0]), "rb1": C(s_rb[1]), "rb2": C(s_rb[2]), "pe": C(c_pe)}

                @blk.tensor
                def _(t):
                    i = t.matmul(pb[3].ap()[:, 0:128], st_onesrow[:], st_gpre.ap()[0:1, c * 128:(c + 1) * 128],
                                 start=True, stop=True)
                    inc(i, c_pe)

                @blk.vector
                def _(v):
                    if c == 0:
                        v.wait_ge(s_rb[0], C(s_rb[0]))
                    v.wait_ge(s_rg, C(s_rg))
                    if c == 0:
                        v.tensor_copy(out=scb_xcw.ap().rearrange("p (a m) -> p a m", a=64)[:, :, 0:3],
                                      in_=st_patchb.ap().rearrange("p (a q) -> p a q", a=80)[:, 0:64, :])
                        v.tensor_copy(out=scb_bcw.ap().rearrange("p (a m) -> p a m", a=16)[:, :, 0:3],
                                      in_=st_patchb.ap().rearrange("p (a q) -> p a q", a=80)[:, 64:80, :])
                    v.wait_ge(c_pe, b0["pe"] + 1)
                    v.tensor_tensor(out=scb_tmp.ap().rearrange("p (h m) -> p h m", h=128),
                                    in0=scb_pini.ap().rearrange("p (h m) -> p h m", h=128),
                                    in1=pb[3].ap()[:, 0:128].unsqueeze(2).broadcast_to([128, 128, 64]),
                                    op=OP.mult)
                    i = v.tensor_add(out=scb_plrb[:], in0=scb_plrb[:], in1=scb_tmp[:])
                    inc(i, c_ve)

            b2 = {"pe": C(c_pe), "se": C(c_se), "ve": C(c_ve), "cv": C(c_cv)}
            with nc.Block(f"sb{c}h") as blk:
                @blk.tensor
                def _(t):
                    t.wait_ge(s_rg, C(s_rg))

                    def ydo(qd):
                        q2 = (qd // 2) % 2
                        gd = qd // 4
                        if qd == 0:
                            t.wait_ge(c_ve, b2["ve"])
                        t.wait_ge(c_cv, b2["cv"] + qd + 1)
                        if qd >= 4:
                            t.wait_ge(c_ve, b2["ve"] + MK["y8"][qd // 2 - 2])
                        for hq in range(4):
                            h = qd * 4 + hq
                            t.matmul(pb[4 + q2].ap()[:, (h % 8) * 64:((h % 8) + 1) * 64],
                                     scb_gml[qd % 2][:, hq * 128:(hq + 1) * 128],
                                     scb_xs.ap()[:, h * 64:(h + 1) * 64], start=True, stop=True)
                            i = t.matmul(pb[6 + q2].ap()[:, (h % 8) * 64:((h % 8) + 1) * 64],
                                         scb_bcw.ap()[:, (8 + gd) * 128:(8 + gd + 1) * 128],
                                         scb_plrb.ap()[:, h * 64:(h + 1) * 64], start=True, stop=True)
                        inc(i, c_pe)

                    for g in range(8):
                        if g > 0:
                            t.wait_ge(c_se, b2["se"] + MK["gmtev"][g - 1])
                        i = t.matmul(pb[2].ap()[:, 0:128],
                                     scb_bcw.ap()[:, g * 128:(g + 1) * 128],
                                     scb_bcw.ap()[:, (8 + g) * 128:(8 + g + 1) * 128],
                                     start=True, stop=True)
                        inc(i, c_pe)
                        for qq in range(4):
                            q = g * 4 + qq
                            if q >= 2:
                                t.wait_ge(c_ve, b2["ve"] + MK["lpre"][q - 2])
                            for hq in range(4):
                                h = q * 4 + hq
                                i = t.matmul(pb[q % 2].ap()[:, hq * 128:(hq + 1) * 128],
                                             st_identf.ap()[:, h:h + 1].broadcast_to([128, 128]),
                                             st_cum.ap()[:, c * 128:(c + 1) * 128],
                                             start=True, stop=True)
                            inc(i, c_pe)
                            if q >= 1:
                                ydo(q - 1)
                    ydo(31)

                @blk.scalar
                def _(s):
                    for g in range(8):
                        s.wait_ge(c_pe, b2["pe"] + MK["gmt"][g])
                        i = s.copy(out=scb_gmt[g % 2][:], in_=pb[2].ap()[:, 0:128])
                        inc(i, c_se)
                        for qq in range(4):
                            q = g * 4 + qq
                            s.wait_ge(c_ve, b2["ve"] + MK["lpre"][q])
                            i = s.activation(out=scb_lt[q % 2][:], in_=scb_lpre[q % 2][:], func=AF.Exp)
                            inc(i, c_se)

                @blk.vector
                def _(v):
                    for q in range(32):
                        g = q // 4
                        v.wait_ge(c_pe, b2["pe"] + MK["cumbc"][q])
                        for hq in range(4):
                            h = q * 4 + hq
                            i = v.scalar_tensor_tensor(out=scb_lpre[q % 2][:, hq * 128:(hq + 1) * 128],
                                                       in0=pb[q % 2].ap()[:, hq * 128:(hq + 1) * 128],
                                                       scalar=st_cumT.ap()[:, c * 128 + h: c * 128 + h + 1],
                                                       in1=st_maskneg[:], op0=OP.subtract, op1=OP.add)
                        i.then_inc(c_ve, 1)
                        cnt[id(c_ve)] += 1
                        v.wait_ge(c_se, b2["se"] + MK["lt"][q])
                        i = v.tensor_tensor(out=scb_gml[q % 2].ap().rearrange("p (a m) -> p a m", a=4),
                                            in0=scb_gmt[g % 2].ap().unsqueeze(1).broadcast_to([128, 4, 128]),
                                            in1=scb_lt[q % 2].ap().rearrange("p (a m) -> p a m", a=4),
                                            op=OP.mult)
                        i.then_inc(c_cv, 1)
                        cnt[id(c_cv)] += 1
                        if q % 2 == 1:
                            bq = (q - 1) // 2
                            q2 = bq % 2
                            v.wait_ge(c_pe, b2["pe"] + MK["ydo"][bq * 2 + 1])
                            i = v.tensor_tensor(out=scb_t8.ap().rearrange("p (h m) -> p h m", h=8),
                                                in0=pb[6 + q2].ap().rearrange("p (h m) -> p h m", h=8),
                                                in1=st_sdT.ap()[:, c * 128 + bq * 8: c * 128 + bq * 8 + 8].unsqueeze(2).broadcast_to([128, 8, 64]),
                                                op=OP.mult)
                            ss(v, i, v_self)
                            i = v.tensor_tensor(out=scb_tmp.ap()[:, bq * 512:(bq + 1) * 512],
                                                in0=scb_t8[:], in1=pb[4 + q2].ap()[:], op=OP.add)
                            i.then_inc(c_ve, 1)
                            cnt[id(c_ve)] += 1
            nc.all_engine_barrier()

            b3 = {"pe": C(c_pe), "ve": C(c_ve)}
            with nc.Block(f"sb{c}y") as blk:
                @blk.tensor
                def _(t):
                    for a in range(NXT):
                        if a >= 4:
                            t.wait_ge(c_ve, b3["ve"] + a - 3)
                        i = t.transpose(pb[4 + a % 4].ap().bitcast(BF16)[:, 0:128],
                                        scb_tmp.ap()[:, a * 128:(a + 1) * 128], st_identbf[:])
                        inc(i, c_pe)

                @blk.vector
                def _(v):
                    v.wait_ge(s_rb[0], C(s_rb[0]))
                    for a in range(NXT):
                        v.wait_ge(c_pe, b3["pe"] + a + 1)
                        i = v.scalar_tensor_tensor(out=scb_ygt.ap()[:, a * 128:(a + 1) * 128],
                                                   in0=scb_xcw.ap()[:, a * 128:(a + 1) * 128],
                                                   scalar=st_drep.ap()[:, a:a + 1],
                                                   in1=pb[4 + a % 4].ap().bitcast(BF16)[:, 0:128],
                                                   op0=OP.mult, op1=OP.add)
                        i.then_inc(c_ve, 1)
                    cnt[id(c_ve)] += NXT

            b4 = {"pe": C(c_pe), "se": C(c_se), "ve": C(c_ve)}
            with nc.Block(f"sb{c}t") as blk:
                @blk.vector
                def _(v):
                    v.wait_ge(s_rb[1], C(s_rb[1]))
                    i = v.tensor_mul(out=scb_ygt[:], in0=scb_ygt[:], in1=scb_gtw[:])
                    inc(i, c_ve)

                @blk.scalar
                def _(s):
                    s.wait_ge(c_ve, b4["ve"] + 1)
                    i = s.activation(out=scb_tmp[:], in_=scb_ygt[:], func=AF.Square)
                    inc(i, c_se)
                    s.wait_ge(c_pe, b4["pe"] + 1)
                    s.copy(out=st_gnr.ap()[0:1, 0:512], in_=pb[2].ap()[0:1, :])
                    i = s.copy(out=st_gnr.ap()[0:1, 512:1024], in_=pb[3].ap()[0:1, :])
                    ss(s, i, s_self)
                    i = s.activation(out=st_gnr[:], in_=st_gnr[:], func=AF.Ln, scale=1.0 / 1024.0, bias=st_konst.ap()[0:1, 1:2])
                    ss(s, i, s_self)
                    i = s.activation(out=st_gnr[:], in_=st_gnr[:], func=AF.Exp, scale=-0.5)
                    inc(i, c_se)

                @blk.tensor
                def _(t):
                    t.wait_ge(c_se, b4["se"] + 1)
                    for g in range(8):
                        bank = pb[2 + g // 4]
                        for kk in range(8):
                            i = t.matmul(bank.ap()[0:1, (g % 4) * 128:(g % 4 + 1) * 128],
                                         st_onesbf[:], scb_tmp.ap()[:, (g * 8 + kk) * 128:(g * 8 + kk + 1) * 128],
                                         start=(kk == 0), stop=(kk == 7))
                    inc(i, c_pe)
                    t.wait_ge(c_se, b4["se"] + 2)
                    for g in range(8):
                        if g >= 2:
                            t.wait_ge(c_ve, b4["ve"] + 1 + g - 1)
                        i = t.matmul(pb[4 + g % 2].ap()[:, 0:128], st_onesrow[:],
                                     st_gnr.ap()[0:1, g * 128:(g + 1) * 128], start=True, stop=True)
                        inc(i, c_pe)

                @blk.vector
                def _(v2):
                    for g in range(8):
                        v2.wait_ge(c_pe, b4["pe"] + 1 + g + 1)
                        for aa in range(8):
                            a = g * 8 + aa
                            i = v2.scalar_tensor_tensor(out=scb_ygt.ap()[:, a * 128:(a + 1) * 128],
                                                        in0=scb_ygt.ap()[:, a * 128:(a + 1) * 128],
                                                        scalar=st_nw.ap()[:, a:a + 1],
                                                        in1=pb[4 + g % 2].ap()[:, 0:128],
                                                        op0=OP.mult, op1=OP.mult)
                        i.then_inc(c_ve, 1)
                        cnt[id(c_ve)] += 1

                @blk.sync
                def _(sy):
                    if c + 1 < NCH:
                        sy.wait_ge(c_ve, b4["ve"] + 1)
                        scb_rb_gtw(sy, c + 1)
                        scb_rb_xcw(sy, c + 1)
                    sy.wait_ge(c_ve, C(c_ve))
                    for p in range(8):
                        i = sy.dma_start(out=ysp.rearrange("a p m -> p a m")[:, 8 * p:8 * (p + 1), c * 128:(c + 1) * 128],
                                         in_=scb_ygt.ap().rearrange("p (a m) -> p a m", a=64)[:, 8 * p:8 * (p + 1), :])
                        inc(i, s_rb[3], 16)
                    sy.wait_ge(s_rb[3], C(s_rb[3]))

                @blk.gpsimd
                def _(g):
                    if c + 1 < NCH:
                        g.wait_ge(c_pe, b4["pe"])
                        scb_rb_small(g, c + 1)
            nc.all_engine_barrier()

    def emit_op(half):
        NS = 16
        opw3 = opwT.rearrange("(a p) m -> p a m", p=128)
        b_sd, b_se, b_ve = C(c_sd), C(c_se), C(c_ve)
        b_wl = [C(s) for s in s_wl]
        b_xl = [C(s) for s in s_xl]
        b_sp = [C(s) for s in s_sp]
        b_rb = C(s_rb[0])
        kbase = half * 32

        with nc.Block(f"op{half}") as blk:
            @blk.sync
            def _(sy0):
                for a in range(32):
                    i = sy0.dma_start(out=xln.ap()[:, a * 1024:(a + 1) * 1024], in_=ysp[kbase + a])
                    inc(i, s_rb[0], 16)

            @blk.gpsimd
            def _(g):
                def wl_static(u, par):
                    for p in range(4):
                        g.dma_start(out=ws[par].ap().rearrange("p (a m) -> p a m", a=32)[:, 8 * p:8 * (p + 1), :],
                                    in_=opw3[:, 8 * p + kbase:8 * (p + 1) + kbase, u * 256:u * 256 + 256]).then_inc(s_wl[par], 16)
                    cnt[id(s_wl[par])] += 64
                for u in range(2):
                    wl_static(u, u)
                for u in range(2, NS):
                    g.wait_ge(c_sd, b_sd + u - 1)
                    wl_static(u, u % 2)

            @blk.tensor
            def _(t):
                gate = c_se if half == 0 else c_ve
                b_gate = b_se if half == 0 else b_ve
                bump(t, gate, 4)
                t.wait_ge(s_rb[0], b_rb + 16 * 32)
                with t.register(f"top{half}") as r:
                    with t.Fori(0, NS // 2) as s2:
                        for par in range(2):
                            aw(t, r, (s2, 64), b_wl[par] + 64, s_wl[par])
                            aw(t, r, (s2, 8), b_gate + 4 + 4 * par - 4, gate)
                            for j in range(2):
                                for k in range(32):
                                    for ts in range(2):
                                        mm = t.matmul(pb[par * 4 + j * 2 + ts][:],
                                                      ws[par][:, k * 256 + j * 128: k * 256 + j * 128 + 128],
                                                      xln[:, k * 1024 + ts * 512: k * 1024 + ts * 512 + 512],
                                                      start=(k == 0), stop=(k == 31))
                            mm.then_inc(c_sd, 1)
                cnt[id(c_sd)] += NS

            if half == 0:
                @blk.scalar
                def _(s):
                    bump(s, s_sp[0], 32)
                    bump(s, s_sp[1], 32)
                    with s.register("sop0") as r:
                        with s.Fori(0, NS // 2) as s2:
                            for par in range(2):
                                aw(s, r, (s2, 2), b_sd + par + 1, c_sd)
                                aw(s, r, (s2, 64), b_sp[1 - par] + 64 * par, s_sp[1 - par])
                                for j in range(2):
                                    for ts in range(2):
                                        q = j * 2 + ts
                                        i = s.activation(out=ops_ps(par * 4 + q),
                                                         in_=pb[par * 4 + j * 2 + ts].ap()[:], func=AF.Copy)
                                        i.then_inc(c_se, 1)
                    cnt[id(c_se)] += NS * 4

                @blk.sync
                def _(sy):
                    for u in range(NS):
                        par = u % 2
                        for j in range(2):
                            for ts in range(2):
                                q = j * 2 + ts
                                sy.wait_ge(c_se, b_se + 4 + 4 * u + q + 1)
                                sy.dma_start(out=psp[u * 2 + j][:, ts * 512:(ts + 1) * 512],
                                             in_=ops_ps(par * 4 + q)).then_inc(s_sp[par], 16)
                        cnt[id(s_sp[par])] += 64
                    sy.wait_ge(s_sp[0], C(s_sp[0]))
                    sy.wait_ge(s_sp[1], C(s_sp[1]))
            else:
                @blk.sync
                def _(sy):
                    def ldpair(u, par):
                        for j in range(2):
                            for ts in range(2):
                                q = j * 2 + ts
                                sy.dma_start(out=ops_ps(par * 4 + q),
                                             in_=psp[u * 2 + j][:, ts * 512:(ts + 1) * 512]).then_inc(s_xl[par], 16)
                                sy.dma_start(out=ops_x1(par * 4 + q),
                                             in_=x1sp[u * 2 + j][:, ts * 512:(ts + 1) * 512]).then_inc(s_xl[par], 16)
                        cnt[id(s_xl[par])] += 128

                    for u in range(2):
                        ldpair(u, u)
                    for u in range(NS):
                        par = u % 2
                        for j in range(2):
                            for ts in range(2):
                                sy.wait_ge(c_ve, b_ve + 4 + 4 * u + j * 2 + ts + 1)
                                sy.dma_start(out=x2sp[u * 2 + j][:, ts * 512:(ts + 1) * 512],
                                             in_=ops_ps(par * 4 + j * 2 + ts)).then_inc(s_sp[par], 16)
                        cnt[id(s_sp[par])] += 64
                        if u + 2 < NS:
                            ldpair(u + 2, par)
                    sy.wait_ge(s_sp[0], C(s_sp[0]))
                    sy.wait_ge(s_sp[1], C(s_sp[1]))

                @blk.vector
                def _(v):
                    with v.register("vop1") as r:
                        with v.Fori(0, NS // 2) as s2:
                            for par in range(2):
                                aw(v, r, (s2, 2), b_sd + par + 1, c_sd)
                                for j in range(2):
                                    for ts in range(2):
                                        q = j * 2 + ts
                                        aw(v, r, (s2, 128), b_xl[par] + 32 * q + 32, s_xl[par])
                                        i = v.tensor_add(out=ops_ps(par * 4 + q), in0=ops_ps(par * 4 + q),
                                                         in1=pb[par * 4 + q].ap()[:])
                                        ss(v, i, v_self)
                                        i = v.tensor_add(out=ops_ps(par * 4 + q), in0=ops_ps(par * 4 + q),
                                                         in1=ops_x1(par * 4 + q))
                                        i.then_inc(c_ve, 1)
                    cnt[id(c_ve)] += NS * 4
        nc.all_engine_barrier()

    import os as _os
    _cut = _os.environ.get("K_CUT", "")
    ipw3 = ipwT.rearrange("(a p) m -> p a m", p=128)

    def pf_ipg():
        base = [C(s) for s in s_wl]
        with nc.Block("pfipg") as blk:
            @blk.gpsimd
            def _(g):
                for u in range(2):
                    for p in range(4):
                        g.dma_start(out=ws[u].ap().rearrange("p (a m) -> p a m", a=32)[:, 8 * p:8 * (p + 1), :],
                                    in_=ipw3[:, 8 * p:8 * (p + 1), u * 256:u * 256 + 256]).then_inc(s_wl[u], 16)
                    cnt[id(s_wl[u])] += 64
        return base

    def pf_ffn():
        base = [C(s) for s in s_wl]
        wre = wbT.rearrange("(a p) m -> p a m", p=128)
        with nc.Block("pfffn") as blk:
            @blk.gpsimd
            def _(g):
                for u in range(2):
                    for p in range(4):
                        g.dma_start(out=ws[u].ap().rearrange("p (a m) -> p a m", a=32)[:, 8 * p:8 * (p + 1), :],
                                    in_=wre[:, 8 * p:8 * (p + 1), u * 256:u * 256 + 256]).then_inc(s_wl[u], 16)
                    g.dma_start(out=stg_ball[u][:], in_=ballT[:, u * 256:u * 256 + 256]).then_inc(s_wl[u], 16)
                    cnt[id(s_wl[u])] += 80
        return base

    def _phases():
        yield "ln1"; emit_ln("ln1", xT3, sb["xln"])
        yield "aux1"; emit_aux("f1")
        yield "ffn1"; emit_ffn("f1", wbT, xT3, x1sp)
        yield "ln2"; emit_ln("ln2", x1sp, sb["xln"])
        yield "ipg"; emit_ip_gate()
        yield "ipx"; emit_ip_xbc()
        yield "dt"; emit_dt()
        yield "ctx"; emit_ctx()
        yield "sca"; emit_sca()
        yield "pex"; emit_pex()
        yield "scb"; emit_scb()
        yield "op0"; emit_op(0)
        yield "op1"; emit_op(1)
        yield "ln3"; emit_ln("ln3", x2sp, sb["xln"])
        yield "aux2"; _pre_f2 = pf_ffn(); emit_aux("f2", reload_a=True)
        yield "ffn2"; emit_ffn("f2", wbT, x2sp, out3, pre=_pre_f2)

    _g = _phases()
    for _name in _g:
        if _cut and _name == _cut:
            break
    es.close()
    nc._sb = sb
    return nc


_NC_CACHE = {}


def _prep(inputs):
    f32 = np.float32
    x = np.asarray(inputs["L0_attn_hidden_states"], f32)
    ipw = np.asarray(inputs["L0_attn_in_proj_w"], f32)
    cw = np.asarray(inputs["L0_attn_conv_w"], f32).reshape(CONV, 4)
    cb = np.asarray(inputs["L0_attn_conv_b"], f32)
    dtb = np.asarray(inputs["L0_attn_dt_bias"], f32)
    alog = np.asarray(inputs["L0_attn_A_log"], f32)
    D = np.asarray(inputs["L0_attn_D"], f32)
    nw = np.asarray(inputs["L0_attn_norm_w"], f32)
    opw = np.asarray(inputs["L0_attn_out_proj_w"], f32)
    wb = np.asarray(inputs["L0_ffn_W_base"], f32)
    bb = np.asarray(inputs["L0_ffn_b_base"], f32)
    la = np.asarray(inputs["L0_ffn_lora_A"], f32)
    lb = np.asarray(inputs["L0_ffn_lora_B"], f32)
    rw = np.asarray(inputs["L0_ffn_router_W"], f32)
    rb = np.asarray(inputs["L0_ffn_router_b"], f32)
    dp = np.asarray(inputs["L0_ffn_domain_prior"], f32)
    act = np.asarray(inputs["L0_ffn_active"], np.int32)

    C_ = np.ascontiguousarray
    shared = {
        "wbT": C_(wb.T),
        "ipwT": C_(ipw.T),
        "opwT": C_(opw.T),
        "aallT": C_(la.transpose(2, 0, 1).reshape(HID, 128)),
        "ballT": C_(lb.transpose(0, 2, 1).reshape(128, HID)),
        "rwT": C_(rw.T),
        "bbcol": C_(bb.reshape(HT, 128).T),
        "prior_c": C_(dp.reshape(16, 1)),
        "rb_c": C_(rb.reshape(16, 1)),
        "iota16": np.arange(16, dtype=f32).reshape(1, 16),
        "activ": C_(act.reshape(1, 4)),
        "rexp": (lambda m: m)(np.zeros((16, 128), f32)),
        "identbf": np.eye(128).astype(BF),
        "identf": np.eye(128, dtype=f32),
        "onesbf": np.ones((128, 1), BF),
        "onesrow": np.ones((1, 128), f32),
        "ones16": np.ones((16, 1), f32),
        "masknegf": np.where(np.arange(128)[None, :] >= np.arange(128)[:, None], 0.0, -1e30).astype(f32),
        "cwT": C_(cw.reshape(80, 128, 4).transpose(1, 0, 2).reshape(128, 320)),
        "cbT": C_(cb.reshape(80, 128).T),
        "dtb_c": C_(dtb.reshape(128, 1)),
        "alog_c": C_(alog.reshape(128, 1)),
        "drep": C_(np.repeat(D, 64).reshape(NXT, 128).T),
        "nwT": C_(nw.reshape(NXT, 128).T),
    }
    for e in range(16):
        shared["rexp"][e, e * 8:(e + 1) * 8] = 2.0
    in_maps = []
    for core in range(8):
        b, h = core // 2, core % 2
        m = dict(shared)
        m["xT"] = C_(x[b, h * T:(h + 1) * T, :].T)
        m["cmask"] = np.full((128, 1), 1.0 if h == 1 else 0.0, f32)
        in_maps.append(m)
    return in_maps


def kernel(**inputs):
    in_maps = _prep(inputs)
    if "nc" not in _NC_CACHE:
        _NC_CACHE["nc"] = build()
    nc = _NC_CACHE["nc"]
    res = run_bass_kernel_spmd(nc, in_maps, core_ids=list(range(8)))
    x = np.asarray(inputs["L0_attn_hidden_states"])
    out = np.empty((4, 2048, HID), np.float32)
    for core in range(8):
        b, h = core // 2, core % 2
        out[b, h * T:(h + 1) * T, :] = res.results[core]["outT"].T
    return out
